# version 08
# baseline (speedup 1.0000x reference)
import numpy as np
import concourse.bass as bass
import concourse.mybir as mybir
from concourse.bass_utils import run_bass_kernel_spmd
from contextlib import ExitStack

F32 = mybir.dt.float32
BF16 = mybir.dt.bfloat16
AF = mybir.ActivationFunctionType
ALU = mybir.AluOpType
AX = mybir.AxisListType

T = 2048
D = 1024
HF = 4096
NCH = T // 128
DCH = D // 128
JCH = HF // 128
GAMMA = 0.96875
NEXP = 8
FEXP = 2048

PHASES = "all"

ENGS = ("sy", "gp", "ve", "sc", "te")


class Bld:

    def __init__(self, sem_names):
        self.ops = {e: [] for e in ENGS}
        self.sem_names = list(sem_names)
        self.sems = {}
        self.cnt = {s: 0 for s in sem_names}
        self.wm = {e: {} for e in ENGS}

    def wait(self, eng, sem, val):
        if val is None or val <= 0:
            return
        if sem == eng:
            return
        if val <= self.wm[eng].get(sem, 0):
            return
        self.wm[eng][sem] = val
        self.ops[eng].append(lambda E, s=sem, v=val: E.wait_ge(self.sems[s], v))

    def op(self, eng, fn, inc=None, n=1):
        if inc is not None:
            self.cnt[inc] += n
            def run(E, fn=fn, s=inc, n=n):
                fn(E).then_inc(self.sems[s], n)
            self.ops[eng].append(run)
            return self.cnt[inc]
        self.ops[eng].append(lambda E, fn=fn: fn(E))
        return None

    def barrier(self):
        for e in ENGS:
            for s in self.sem_names:
                self.wait(e, s, self.cnt[s])

    def dma(self, eng, qsem, out, in_, slow=False):
        if slow:
            return self.op(eng, lambda E, o=out, i=in_: E.dma_start(
                out=o, in_=i, allow_slow_non_contiguous=True), inc=qsem, n=16)
        return self.op(eng, lambda E, o=out, i=in_: E.dma_start(out=o, in_=i),
                       inc=qsem, n=16)


def _host_constants():
    d = D
    half = d // 2
    pos = np.arange(T, dtype=np.float64)
    scale_vec = (np.arange(0, d, 2, dtype=np.float64) + 0.4 * d) / (1.4 * d)
    scale = scale_vec[None, :] ** (pos[:, None] / 512.0)
    inv_freq = 1.0 / 10000.0 ** (np.arange(half, dtype=np.float64) / half)
    sinus = pos[:, None] * inv_freq[None, :]
    sin_u = np.repeat(np.sin(sinus) * scale, 2, axis=-1)
    cos_u = np.repeat(np.cos(sinus) * scale, 2, axis=-1)
    sin_d = np.repeat(np.sin(sinus) / scale, 2, axis=-1)
    cos_d = np.repeat(np.cos(sinus) / scale, 2, axis=-1)
    gpos = GAMMA ** pos
    gneg = GAMMA ** (-pos)

    def fold(s):
        out = s.copy()
        out[:, 0::2] = -s[:, 0::2]
        return out

    import ml_dtypes
    cst = dict(
        tqc=(cos_u * gpos[:, None]).astype(np.float32),
        tqs=(fold(sin_u) * gpos[:, None]).astype(np.float32),
        tkc=(cos_d * gneg[:, None]).astype(np.float32),
        tks=(fold(sin_d) * gneg[:, None]).astype(np.float32),
        ident=np.eye(128, dtype=ml_dtypes.bfloat16),
    )
    q = np.arange(512)[None, :]
    p = np.arange(128)[:, None]
    cst["masks"] = np.concatenate(
        [(q >= 128 * r + p) for r in range(4)], 0).astype(ml_dtypes.bfloat16)
    cst["ones1"] = np.ones((1, 128), dtype=np.float32)
    cst["onesc"] = np.ones((128, 1), dtype=np.float32)
    consts = np.zeros((128, 16), dtype=np.float32)
    consts[:, 0] = 1e-5
    consts[:, 1] = 1e-6
    cst["consts"] = consts
    return cst


def build(nc, dbg=False):
    X = nc.dram_tensor("x_in", [T, D], F32, kind="ExternalInput").ap()
    WQ = nc.dram_tensor("w_q", [D, D], F32, kind="ExternalInput").ap()
    WK0 = nc.dram_tensor("w_k0", [D, D], F32, kind="ExternalInput").ap()
    WV0 = nc.dram_tensor("w_v0", [D, D], F32, kind="ExternalInput").ap()
    WFC = nc.dram_tensor("w_fc", [HF, D], F32, kind="ExternalInput").ap()
    LNWI = nc.dram_tensor("ln_w", [HF], F32, kind="ExternalInput").ap()
    LNBI = nc.dram_tensor("ln_b", [HF], F32, kind="ExternalInput").ap()
    WPJ = nc.dram_tensor("w_proj", [D, HF], F32, kind="ExternalInput").ap()
    WK1 = nc.dram_tensor("w_k1", [D, D], F32, kind="ExternalInput").ap()
    W1A = nc.dram_tensor("w1a", [64, D], F32, kind="ExternalInput").ap()
    W2A = nc.dram_tensor("w2a", [D, 64], F32, kind="ExternalInput").ap()
    WG = nc.dram_tensor("wg", [D, NEXP], F32, kind="ExternalInput").ap()
    WE1 = nc.dram_tensor("we1", [NEXP, D, FEXP], F32, kind="ExternalInput").ap()
    WE2 = nc.dram_tensor("we2", [NEXP, FEXP, D], F32, kind="ExternalInput").ap()
    TQC = nc.dram_tensor("tqc", [T, D], F32, kind="ExternalInput").ap()
    TQS = nc.dram_tensor("tqs", [T, D], F32, kind="ExternalInput").ap()
    TKC = nc.dram_tensor("tkc", [T, D], F32, kind="ExternalInput").ap()
    TKS = nc.dram_tensor("tks", [T, D], F32, kind="ExternalInput").ap()
    IDENTI = nc.dram_tensor("ident", [128, 128], BF16, kind="ExternalInput").ap()
    MASKSI = nc.dram_tensor("masks", [4 * 128, 512], BF16, kind="ExternalInput").ap()
    ONES1I = nc.dram_tensor("ones1", [1, 128], F32, kind="ExternalInput").ap()
    ONESCI = nc.dram_tensor("onesc", [128, 1], F32, kind="ExternalInput").ap()
    CONSTI = nc.dram_tensor("consts", [128, 16], F32, kind="ExternalInput").ap()
    Y = nc.dram_tensor("y_out", [T, D], F32, kind="ExternalOutput").ap()
    WFCT = nc.dram_tensor("s_wfct", [8, 128, 8 * 512], BF16, kind="Internal").ap()
    WPJT = nc.dram_tensor("s_wpjt", [8, 128, 4 * 1024], BF16, kind="Internal").ap()
    QFH = nc.dram_tensor("s_qfh", [DCH, 128, T], BF16, kind="Internal").ap()
    QFL = nc.dram_tensor("s_qfl", [DCH, 128, T], BF16, kind="Internal").ap()
    KFH = nc.dram_tensor("s_kfh", [DCH, 128, T], BF16, kind="Internal").ap()
    KFL = nc.dram_tensor("s_kfl", [DCH, 128, T], BF16, kind="Internal").ap()
    VTH = nc.dram_tensor("s_vth", [NCH, 128, D], BF16, kind="Internal").ap()
    VTL = nc.dram_tensor("s_vtl", [NCH, 128, D], BF16, kind="Internal").ap()

    b = Bld(("qw0", "qw1", "qx0", "qx1", "qsp", "qsp1", "qo", "te", "ve", "sc", "gp2"))
    es = ExitStack()
    sbuf = lambda name, shape, dt: es.enter_context(nc.sbuf_tensor(name, shape, dt))
    psum = lambda name, shape, dt: es.enter_context(nc.psum_tensor(name, shape, dt))

    ACC = sbuf("ACC", [128, NCH * D], F32)
    XNH = sbuf("XNH", [128, DCH * T], BF16)
    XNL = sbuf("XNL", [128, DCH * T], BF16)
    PBF = sbuf("PBF", [128, 20480], BF16)
    PF3 = sbuf("PF3", [128, 6144], F32)
    IDE = sbuf("IDE", [128, 128], BF16)
    STAT = sbuf("STAT", [128, 64], F32)
    LNW = sbuf("LNW", [128, 32], F32)
    LNB = sbuf("LNB", [128, 32], F32)
    LNW2 = sbuf("LNW2", [128, 64], BF16)
    CONST = sbuf("CONST", [128, 16], F32)
    ONE1 = sbuf("ONE1", [1, 128], F32)
    ONEC = sbuf("ONEC", [128, 1], F32)
    B12 = sbuf("B12", [1, 2 * D], F32)
    SG1 = sbuf("SG1", [128, 128], F32)
    SG2 = sbuf("SG2", [128, 128], F32)
    CGT = sbuf("CGT", [128, 128], F32)
    GTH = sbuf("GTH", [128, 16], F32)
    VJ = sbuf("VJ", [128, 512], F32)
    PRP = sbuf("PRP", [128, 1024], BF16)

    PH = [psum("PH0", [128, 512], F32), psum("PH1", [128, 512], F32)]
    PY = [psum("PY0", [128, 1024], F32), psum("PY1", [128, 1024], F32)]
    PT = psum("PT", [128, 2048], BF16)

    XNH3 = XNH[:].rearrange("p (e t) -> p e t", e=DCH)
    XNL3 = XNL[:].rearrange("p (e t) -> p e t", e=DCH)

    def accs(c, a=0, n=D):
        return ACC[:, c * D + a: c * D + a + n]

    def vgap(n=1):
        for _ in range(n):
            b.op("ve", lambda E: E.tensor_scalar(VJ[:], VJ[:], 1.0, None, ALU.mult))

    b.dma("sy", "qw0", IDE[:], IDENTI)
    b.dma("sy", "qw0", CONST[:], CONSTI)
    b.dma("sy", "qw0", ONE1[:], ONES1I)
    b.dma("sy", "qw0", ONEC[:], ONESCI)
    b.dma("sy", "qw0", LNW[:], LNWI.rearrange("(c p) -> p c", p=128), slow=True)
    c_init = b.dma("sy", "qw0", LNB[:], LNBI.rearrange("(c p) -> p c", p=128), slow=True)

    b.op("ve", lambda E: E.memset(VJ[:], 0.0))
    b.wait("ve", "qw0", c_init)
    LNW2r = LNW2[:].rearrange("p (c k) -> p c k", k=2)
    b.op("ve", lambda E: E.tensor_copy(LNW2r[:, :, 0:1], LNW[:].rearrange("p (c k) -> p c k", k=1)))
    v_lnw2 = b.op("ve", lambda E: E.tensor_copy(
        LNW2r[:, :, 1:2], LNB[:].rearrange("p (c k) -> p c k", k=1)), inc="ve")

    XT = [PF3[:, 0:1024], PF3[:, 1024:2048]]
    SQJ = PF3[:, 2048:3072]

    def stats_finalize(n_feat):
        b.op("ve", lambda E: E.tensor_scalar(STAT[:, 32:48], STAT[:, 0:16], 1.0 / n_feat, None, ALU.mult))
        b.op("ve", lambda E: E.tensor_scalar(STAT[:, 48:64], STAT[:, 16:32], 1.0 / n_feat, None, ALU.mult))
        vgap()
        b.op("ve", lambda E: E.tensor_tensor(STAT[:, 0:16], STAT[:, 32:48], STAT[:, 32:48], ALU.mult))
        vgap()
        vv = b.op("ve", lambda E: E.tensor_tensor(STAT[:, 48:64], STAT[:, 48:64], STAT[:, 0:16], ALU.subtract), inc="ve")
        b.wait("sc", "ve", vv)
        sv = b.op("sc", lambda E: E.activation(STAT[:, 48:64], STAT[:, 48:64], AF.Sqrt, bias=CONST[:, 0:1]), inc="sc")
        b.wait("ve", "sc", sv)
        b.op("ve", lambda E: E.reciprocal(STAT[:, 48:64], STAT[:, 48:64]))
        vgap()
        b.op("ve", lambda E: E.tensor_tensor(STAT[:, 0:16], STAT[:, 32:48], STAT[:, 48:64], ALU.mult))
        vgap()
        return b.op("ve", lambda E: E.tensor_scalar(STAT[:, 0:16], STAT[:, 0:16], -1.0, None, ALU.mult), inc="ve")

    def ln_stats(from_hbm, pump=None):
        xt_free = [None, None]
        for c in range(NCH):
            if pump is not None:
                pump(c)
            pb = c % 2
            if from_hbm:
                q = "qx0" if pb == 0 else "qx1"
                if xt_free[pb]:
                    b.wait("sy", *xt_free[pb])
                cq = b.dma("sy", q, XT[pb][:], X[c * 128:(c + 1) * 128, :])
                src = XT[pb][:]
                b.wait("ve", q, cq)
            else:
                src = accs(c)
            b.op("ve", lambda E, c=c, src=src: E.tensor_reduce(
                STAT[:, c:c + 1], src, AX.X, ALU.add), inc="ve")
            b.op("ve", lambda E, src=src: E.tensor_tensor(SQJ[:], src, src, ALU.mult))
            v = b.op("ve", lambda E, c=c: E.tensor_reduce(
                STAT[:, 16 + c:17 + c], SQJ[:], AX.X, ALU.add), inc="ve")
            xt_free[pb] = ("ve", v)
        vgap()
        return stats_finalize(D)

    def ln_passB(from_hbm, v_stats):
        XNT = [PF3[:, 2048:3072], PF3[:, 3072:4096]]
        XHb = [PBF[:, 0:1024], PBF[:, 2048:3072]]
        XLb = [PBF[:, 1024:2048], PBF[:, 3072:4096]]
        xnt_free = [("ve", v_stats), ("ve", v_stats)]
        xh_free = [("te", 0), ("te", 0)]
        pt_free = ("ve", 0)
        xt_free = [("ve", v_stats), ("ve", v_stats)]
        b.wait("sc", "ve", v_stats)
        last_ve = 0
        for c in range(NCH):
            pb = c % 2
            if from_hbm:
                q = "qx0" if pb == 0 else "qx1"
                b.wait("sy", *xt_free[pb])
                cq = b.dma("sy", q, XT[pb][:], X[c * 128:(c + 1) * 128, :])
                b.wait("sc", q, cq)
                src = XT[pb][:]
            else:
                src = accs(c)
            b.wait("sc", *xnt_free[pb])
            s_norm = b.op("sc", lambda E, c=c, pb=pb, src=src: E.activation(
                XNT[pb][:], src, AF.Identity,
                scale=STAT[:, 48 + c:49 + c], bias=STAT[:, c:c + 1]), inc="sc")
            xt_free[pb] = ("sc", s_norm)
            b.wait("ve", "sc", s_norm)
            b.wait("ve", *xh_free[pb])
            b.op("ve", lambda E, pb=pb: E.tensor_copy(XHb[pb][:], XNT[pb][:]))
            v_split = b.op("ve", lambda E, pb=pb: E.tensor_tensor(
                XLb[pb][:], XNT[pb][:], XHb[pb][:], ALU.subtract), inc="ve")
            xnt_free[pb] = ("ve", v_split)
            b.wait("te", "ve", v_split)
            b.wait("te", *pt_free)
            b.wait("te", "qw0", c_init)
            for e in range(DCH):
                b.op("te", lambda E, e=e, pb=pb: E.transpose(
                    PT[:, e * 128:(e + 1) * 128],
                    XHb[pb][:, e * 128:(e + 1) * 128], IDE[:]))
            for e in range(DCH):
                kw = {"inc": "te"} if e == DCH - 1 else {}
                b.op("te", lambda E, e=e, pb=pb: E.transpose(
                    PT[:, 1024 + e * 128:1024 + (e + 1) * 128],
                    XLb[pb][:, e * 128:(e + 1) * 128], IDE[:]), **kw)
            t_tr = b.cnt["te"]
            xh_free[pb] = ("te", t_tr)
            b.wait("ve", "te", t_tr)
            b.op("ve", lambda E, c=c: E.tensor_copy(
                XNH3[:, :, c * 128:(c + 1) * 128],
                PT[:, 0:1024].rearrange("p (e t) -> p e t", e=DCH)))
            last_ve = b.op("ve", lambda E, c=c: E.tensor_copy(
                XNL3[:, :, c * 128:(c + 1) * 128],
                PT[:, 1024:2048].rearrange("p (e t) -> p e t", e=DCH)), inc="ve")
            pt_free = ("ve", last_ve)
        return last_ve

    W32G = [PF3[:, 4096:5120], PF3[:, 5120:6144]]
    gam_free = [("ve", 0), ("ve", 0)]

    def gamma_tile(k):
        if k < 32:
            ap = WFC[k * 128:(k + 1) * 128, :]
            gs = 0 + k
        else:
            kk = k - 32
            ap = WPJ[(kk // 4) * 128:(kk // 4 + 1) * 128, (kk % 4) * 1024:(kk % 4 + 1) * 1024]
            gs = 40 + kk
        pb = k % 2
        q = "qw0" if pb == 0 else "qw1"
        b.wait("sy", *gam_free[pb])
        cq = b.dma("sy", q, W32G[pb][:], ap)
        b.wait("ve", q, cq)
        v = b.op("ve", lambda E, gs=gs, pb=pb: E.tensor_reduce(
            SG2[:, gs:gs + 1], W32G[pb][:], AX.X, ALU.add,
            apply_absolute_value=True), inc="ve")
        gam_free[pb] = ("ve", v)

    def gamma_fin(col, gs_col):
        vgap()
        v = b.op("ve", lambda E, gs=gs_col: E.tensor_reduce(
            GTH[:, 4:5], SG2[:, gs:gs + 32], AX.X, ALU.add), inc="ve")
        vgap()
        b.wait("te", "ve", b.cnt["ve"])
        t1 = b.op("te", lambda E: E.matmul(PH[0][0:1, 0:1], ONEC[:], GTH[:, 4:5],
                                           start=True, stop=True), inc="te")
        b.wait("ve", "te", t1)
        v = b.op("ve", lambda E: E.tensor_copy(GTH[0:1, 5:6], PH[0][0:1, 0:1]), inc="ve")
        b.wait("te", "ve", v)
        t2 = b.op("te", lambda E: E.matmul(PH[1][:, 0:1], ONE1[:], GTH[0:1, 5:6],
                                           start=True, stop=True), inc="te")
        b.wait("ve", "te", t2)
        b.op("ve", lambda E: E.tensor_scalar(GTH[:, 6:7], PH[1][:, 0:1], 0.5 / (HF * D), None, ALU.mult))
        vgap()
        b.op("ve", lambda E, col=col: E.tensor_scalar(GTH[:, col:col + 1], GTH[:, 6:7], 0.5e-6, None, ALU.add))
        vgap()
        return b.op("ve", lambda E, col=col: E.tensor_scalar(
            GTH[:, col + 1:col + 2], GTH[:, col:col + 1], -1.0, None, ALU.mult), inc="ve")


    v_stats = ln_stats(True, pump=lambda c: [gamma_tile(4 * c + i) for i in range(4)])
    v_xn = ln_passB(True, v_stats)
    v_tfc = gamma_fin(0, 0)
    v_tpj = gamma_fin(2, 40)
    b.barrier()
    if dbg:
        XNHD = nc.dram_tensor("xnh_dbg", [128, DCH * T], BF16, kind="ExternalOutput").ap()
        XNLD = nc.dram_tensor("xnl_dbg", [128, DCH * T], BF16, kind="ExternalOutput").ap()
        STATD = nc.dram_tensor("stat_dbg", [128, 64], F32, kind="ExternalOutput").ap()
        b.dma("sy", "qo", XNHD, XNH[:])
        b.dma("sy", "qo", XNLD, XNL[:])
        b.dma("sy", "qo", STATD, STAT[:])
        b.wait("sy", "qo", b.cnt["qo"])
        b.barrier()

    W32 = [PF3[:, 4096:5120], PF3[:, 5120:6144]]

    QTB = [PBF[:, 4096:5120], PBF[:, 5120:6144]]
    P12 = [PF3[:, 0:1024], PF3[:, 1024:2048]]
    STG = [PBF[:, 6144:7168], PBF[:, 7168:8192]]

    def quant_tile(pb, thr_col, v_thr):
        b.wait("ve", "ve", v_thr)
        b.op("ve", lambda E, pb=pb, tc=thr_col: E.tensor_scalar(
            P12[0][:], W32[pb][:], GTH[:, tc:tc + 1], None, ALU.is_gt))
        b.op("ve", lambda E, pb=pb, tc=thr_col: E.tensor_scalar(
            P12[1][:], W32[pb][:], GTH[:, tc + 1:tc + 2], None, ALU.is_lt))
        return b.op("ve", lambda E, pb=pb: E.tensor_tensor(
            QTB[pb][:], P12[0][:], P12[1][:], ALU.subtract), inc="ve")

    WFCT4 = WFCT.rearrange("jt p (e j) -> jt p e j", e=DCH)
    b.barrier()
    WPJT4 = WPJT.rearrange("jt p (kk o) -> jt p kk o", kk=4)

    def prep_spill_wpj():
        beta_ev = [0]
        w_free = [("ve", 0), ("ve", 0)]
        qt_free = [("te", 0), ("te", 0)]
        pt_free2 = [("ve", 0), ("ve", 0)]
        stg_free = [[], []]
        for k in range(32):
            pb = k % 2
            jq, ob = k // 8, k % 8
            q = "qx0" if pb == 0 else "qx1"
            b.wait("sy", *w_free[pb])
            cq = b.dma("sy", q, W32[pb][:],
                       WPJ[ob * 128:(ob + 1) * 128, jq * 1024:(jq + 1) * 1024])
            b.wait("ve", q, cq)
            b.wait("ve", *qt_free[pb])
            v_q = quant_tile(pb, 2, v_tpj)
            w_free[pb] = ("ve", v_q)
            b.wait("te", "ve", v_q)
            b.wait("te", *pt_free2[pb])
            for e in range(DCH):
                kw = {"inc": "te"} if e == DCH - 1 else {}
                b.op("te", lambda E, e=e, pb=pb: E.transpose(
                    PT[:, pb * 1024 + e * 128:pb * 1024 + (e + 1) * 128],
                    QTB[pb][:, e * 128:(e + 1) * 128], IDE[:]), **kw)
            t_tr = b.cnt["te"]
            qt_free[pb] = ("te", t_tr)
            b.wait("ve", "te", t_tr)
            for w in stg_free[pb]:
                b.wait("ve", *w)
            v_e = b.op("ve", lambda E, pb=pb: E.tensor_copy(
                STG[pb][:], PT[:, pb * 1024:(pb + 1) * 1024]), inc="ve")
            pt_free2[pb] = ("ve", v_e)
            b.wait("gp", "ve", v_e)
            qs = "qsp" if pb == 0 else "qsp1"
            b.dma("gp", qs,
                  WPJT4[2 * jq, :, 0:4, ob * 128:(ob + 1) * 128],
                  STG[pb][:, 0:512].rearrange("p (e j) -> p e j", e=4))
            cs = b.dma("gp", qs,
                       WPJT4[2 * jq + 1, :, 0:4, ob * 128:(ob + 1) * 128],
                       STG[pb][:, 512:1024].rearrange("p (e j) -> p e j", e=4))
            b.wait("te", "ve", v_e)
            b.wait("te", "ve", v_lnw2)
            b.wait("te", "ve", beta_ev[0])
            for e in range(DCH):
                jc = 8 * jq + e
                b.op("te", lambda E, e=e, jc=jc, pb=pb, ob=ob,
                     st=(e == 0), sp=(e == DCH - 1): E.matmul(
                    PY[0][0:1, ob * 128:(ob + 1) * 128], LNW2[:, 2 * jc:2 * jc + 1],
                    STG[pb][:, e * 128:(e + 1) * 128], start=st, stop=sp))
                kw = {"inc": "te"} if e == DCH - 1 else {}
                b.op("te", lambda E, e=e, jc=jc, pb=pb, ob=ob,
                     st=(e == 0), sp=(e == DCH - 1): E.matmul(
                    PY[1][0:1, ob * 128:(ob + 1) * 128], LNW2[:, 2 * jc + 1:2 * jc + 2],
                    STG[pb][:, e * 128:(e + 1) * 128], start=st, stop=sp), **kw)
            t_beta = b.cnt["te"]
            b.wait("ve", "te", t_beta)
            if jq == 0:
                b.op("ve", lambda E, ob=ob: E.tensor_copy(
                    B12[0:1, ob * 128:(ob + 1) * 128], PY[0][0:1, ob * 128:(ob + 1) * 128]))
                v_bev = b.op("ve", lambda E, ob=ob: E.tensor_copy(
                    B12[0:1, D + ob * 128:D + (ob + 1) * 128],
                    PY[1][0:1, ob * 128:(ob + 1) * 128]), inc="ve")
            else:
                b.op("ve", lambda E, ob=ob: E.tensor_tensor(
                    B12[0:1, ob * 128:(ob + 1) * 128], B12[0:1, ob * 128:(ob + 1) * 128],
                    PY[0][0:1, ob * 128:(ob + 1) * 128], ALU.add))
                v_bev = b.op("ve", lambda E, ob=ob: E.tensor_tensor(
                    B12[0:1, D + ob * 128:D + (ob + 1) * 128],
                    B12[0:1, D + ob * 128:D + (ob + 1) * 128],
                    PY[1][0:1, ob * 128:(ob + 1) * 128], ALU.add), inc="ve")
            beta_ev[0] = v_bev
            stg_free[pb] = [("te", t_beta), (qs, cs)]
        return b.cnt["qsp"]

    b.barrier()
    c_spill = prep_spill_wpj()
    b.barrier()
    v_beta = b.cnt["ve"]

    W32F = PF3[:, 0:1024]
    P2F = PF3[:, 1024:2048]
    QTBF = PRP[:, 0:1024]
    STGF = PRP[:, 0:1024]
    fcp = {"w32": ("ve", 0), "qtb": ("te", 0), "stg": None, "ptp": ("ve", 0),
           "load": 0, "vq": 0, "p2c": 0}

    def fc_stepA(k):
        pb = k % 2
        q = "qw0" if pb == 0 else "qw1"
        b.wait("sy", *fcp["w32"])
        cq = b.dma("sy", q, W32F[:], WFC[k * 128:(k + 1) * 128, :])
        b.wait("ve", q, cq)
        b.wait("ve", *fcp["qtb"])
        if fcp["stg"]:
            b.wait("ve", *fcp["stg"])
        b.op("ve", lambda E: E.tensor_scalar(P2F[:], W32F[:], GTH[:, 1:2], None, ALU.is_lt))
        v_q = b.op("ve", lambda E: E.scalar_tensor_tensor(
            QTBF[:], W32F[:], GTH[:, 0:1], P2F[:], ALU.is_gt, ALU.subtract), inc="ve")
        fcp["w32"] = ("ve", v_q)
        fcp["vq"] = v_q

    def fc_stepB(k):
        b.wait("te", "ve", fcp["vq"])
        b.wait("te", *fcp["ptp"])
        for e in range(DCH):
            kw = {"inc": "te"} if e == DCH - 1 else {}
            b.op("te", lambda E, e=e: E.transpose(
                PT[:, 1024 + e * 128:1024 + (e + 1) * 128],
                QTBF[:, e * 128:(e + 1) * 128], IDE[:]), **kw)
        t_tr = b.cnt["te"]
        fcp["qtb"] = ("te", t_tr)
        b.wait("ve", "te", t_tr)
        if fcp["stg"]:
            b.wait("ve", *fcp["stg"])
        v_e = b.op("ve", lambda E: E.tensor_copy(STGF[:], PT[:, 1024:2048]), inc="ve")
        fcp["ptp"] = ("ve", v_e)
        b.wait("gp", "ve", v_e)
        qs = "qsp" if k % 2 == 0 else "qsp1"
        cs = b.dma("gp", qs, WFCT4[k // 4, :, :, (k % 4) * 128:(k % 4 + 1) * 128],
                   STGF[:].rearrange("p (e j) -> p e j", e=DCH))
        fcp["stg"] = (qs, cs)

    from collections import deque
    fc_steps = deque()
    for _k in range(32):
        fc_steps.append((fc_stepA, _k))
        fc_steps.append((fc_stepB, _k))

    def pump_fc():
        if fc_steps:
            f, k = fc_steps.popleft()
            f(k)

    for _ in range(8):
        pump_fc()

    B1B = PF3[:, 4096:5120]
    B2B = PF3[:, 5120:6144]
    b.wait("te", "ve", v_beta)
    tb = b.op("te", lambda E: E.matmul(PY[0][:, 0:512], ONE1[:], B12[0:1, 0:512], start=True, stop=True))
    tb = b.op("te", lambda E: E.matmul(PY[0][:, 512:1024], ONE1[:], B12[0:1, 512:1024], start=True, stop=True))
    tb = b.op("te", lambda E: E.matmul(PY[1][:, 0:512], ONE1[:], B12[0:1, 1024:1536], start=True, stop=True))
    tb = b.op("te", lambda E: E.matmul(PY[1][:, 512:1024], ONE1[:], B12[0:1, 1536:2048], start=True, stop=True), inc="te")
    b.wait("ve", "te", tb)
    b.op("ve", lambda E: E.tensor_copy(B1B, PY[0][:]))
    v_bb = b.op("ve", lambda E: E.tensor_copy(B2B, PY[1][:]), inc="ve")

    WFB = [PBF[:, 0:4096], PBF[:, 4096:8192]]
    WPB = [PBF[:, 8192:12288], PBF[:, 12288:16384]]
    GHB = [PBF[:, 16384:16896], PBF[:, 16896:17408]]
    GLB = [PBF[:, 17408:17920], PBF[:, 17920:18432]]
    GFH = [PBF[:, 18432:18944], PBF[:, 18944:19456]]
    GFL = [PBF[:, 19456:19968], PBF[:, 19968:20480]]
    GT = [PF3[:, 2048:2560], PF3[:, 2560:3072]]
    SQ2 = PF3[:, 3072:3584]

    wf_free = [("te", 0), ("te", 0)]
    wp_free = [("te", 0), ("te", 0)]
    ph_free = [("ve", v_bb), ("ve", v_bb)]
    gt_free = [("ve", 0), ("ve", 0)]
    gh_free = [("te", 0), ("te", 0)]
    ptf_free = [("ve", 0), ("ve", 0)]
    gf_free = [("te", 0), ("te", 0)]
    py_free = [("ve", v_bb), ("ve", v_bb)]
    z_done = [0] * NCH

    def ffn_tr(jt, c):
        pb = c % 2
        v_sp = split_ms[pb]
        b.wait("te", "ve", v_sp)
        b.wait("te", *ptf_free[0])
        for k in range(4):
            b.op("te", lambda E, k=k, pb=pb: E.transpose(
                PT[:, k * 128:(k + 1) * 128],
                GHB[pb][:, k * 128:(k + 1) * 128], IDE[:]))
        for k in range(4):
            kw = {"inc": "te"} if k == 3 else {}
            b.op("te", lambda E, k=k, pb=pb: E.transpose(
                PT[:, 512 + k * 128:512 + (k + 1) * 128],
                GLB[pb][:, k * 128:(k + 1) * 128], IDE[:]), **kw)
        t_tr = b.cnt["te"]
        gh_free[pb] = ("te", t_tr)
        b.wait("ve", "te", t_tr)
        b.wait("ve", *gf_free[pb])
        for k in range(4):
            jc = 4 * jt + k
            b.op("ve", lambda E, k=k, pb=pb, jc=jc: E.tensor_scalar(
                GFH[pb][:, k * 128:(k + 1) * 128],
                PT[:, k * 128:(k + 1) * 128],
                LNW[:, jc:jc + 1], None, ALU.mult))
        for k in range(4):
            jc = 4 * jt + k
            kw = {"inc": "ve"} if k == 3 else {}
            b.op("ve", lambda E, k=k, pb=pb, jc=jc: E.tensor_scalar(
                GFL[pb][:, k * 128:(k + 1) * 128],
                PT[:, 512 + k * 128:512 + (k + 1) * 128],
                LNW[:, jc:jc + 1], None, ALU.mult), **kw)
        gf_ready[pb] = b.cnt["ve"]
        ptf_free[0] = ("ve", gf_ready[pb])

    def ffn_z(jt, c):
        pb = c % 2
        pj = jt % 2
        b.wait("te", "ve", gf_ready[pb])
        b.wait("te", *py_free[pb])
        for oh in range(2):
            for k in range(4):
                for hp, GSRC in ((0, GFH), (1, GFL)):
                    last = (oh == 1 and k == 3 and hp == 1)
                    kw = {"inc": "te"} if last else {}
                    b.op("te", lambda E, oh=oh, k=k, pb=pb, pj=pj, GSRC=GSRC,
                         st=(k == 0 and hp == 0), sp=(k == 3 and hp == 1): E.matmul(
                        PY[pb][:, oh * 512:(oh + 1) * 512],
                        GSRC[pb][:, k * 128:(k + 1) * 128],
                        WPB[pj][:, k * 1024 + oh * 512:k * 1024 + (oh + 1) * 512],
                        start=st, stop=sp), **kw)
        t_z = b.cnt["te"]
        gf_free[pb] = ("te", t_z)
        wp_free[pj] = ("te", t_z)
        b.wait("ve", "te", t_z)
        if jt == 0:
            v_z = b.op("ve", lambda E, c=c, pb=pb: E.tensor_copy(accs(c), PY[pb][:]), inc="ve")
        else:
            v_z = b.op("ve", lambda E, c=c, pb=pb: E.tensor_tensor(
                accs(c), accs(c), PY[pb][:], ALU.add), inc="ve")
        z_done[c] = v_z
        py_free[pb] = ("ve", v_z)

    split_ms = [0, 0]
    gf_ready = [0, 0]
    pend_tr = None
    pend_z = None
    for jt in range(8):
        pj = jt % 2
        qj = "qw0" if pj == 0 else "qw1"
        b.wait("sy", "qsp", b.cnt["qsp"])
        b.wait("sy", "qsp1", b.cnt["qsp1"])
        b.wait("sy", *wf_free[pj])
        b.wait("sy", *wp_free[pj])
        cw = b.dma("sy", qj, WFB[pj][:], WFCT[jt])
        cw = b.dma("sy", qj, WPB[pj][:], WPJT[jt])
        for c in range(NCH):
            pb = c % 2
            b.wait("te", qj, cw)
            b.wait("te", "ve", v_xn)
            b.wait("te", *ph_free[pb])
            for e in range(DCH):
                for hp, XSRC in ((0, XNH), (1, XNL)):
                    kw = {"inc": "te"} if (e == DCH - 1 and hp == 1) else {}
                    b.op("te", lambda E, e=e, c=c, pb=pb, pj=pj, XSRC=XSRC,
                         st=(e == 0 and hp == 0), sp=(e == DCH - 1 and hp == 1): E.matmul(
                        PH[pb][:], XSRC[:, e * T + c * 128:e * T + (c + 1) * 128],
                        WFB[pj][:, e * 512:(e + 1) * 512], start=st, stop=sp), **kw)
            t_h = b.cnt["te"]
            wf_free[pj] = ("te", t_h)
            if pend_tr is not None:
                ffn_tr(*pend_tr)
            if pend_z is not None:
                ffn_z(*pend_z)
            pend_z = pend_tr
            pump_fc()
            b.wait("sc", "te", t_h)
            b.wait("sc", *gt_free[pb])
            col = c * 8 + jt
            b.op("sc", lambda E, pb=pb, col=col: E.activation(
                GT[pb][:], PH[pb][:], AF.Gelu, accum_out=SG1[:, col:col + 1]))
            s_g = b.op("sc", lambda E, pb=pb, col=col: E.activation(
                SQ2[:], GT[pb][:], AF.Square, accum_out=SG2[:, col:col + 1]), inc="sc")
            ph_free[pb] = ("sc", s_g)
            b.wait("sc", *gh_free[pb])
            s_hi = b.op("sc", lambda E, pb=pb: E.activation(
                GHB[pb][:], GT[pb][:], AF.Identity), inc="sc")
            ph_free[pb] = ("sc", s_hi)
            b.wait("ve", "sc", s_hi)
            v_sp = b.op("ve", lambda E, pb=pb: E.tensor_tensor(
                GLB[pb][:], GT[pb][:], GHB[pb][:], ALU.subtract), inc="ve")
            gt_free[pb] = ("ve", v_sp)
            split_ms[pb] = v_sp
            pend_tr = (jt, c)
    ffn_tr(*pend_tr)
    if pend_z is not None:
        ffn_z(*pend_z)
    ffn_z(*pend_tr)

    for _ in range(8):
        b.op("sc", lambda E: E.activation(GTH[:, 9:10], CONST[:, 0:1], AF.Square))
    b.op("sc", lambda E: E.activation(GTH[:, 9:10], CONST[:, 0:1], AF.Square), inc="sc")
    b.barrier()
    b.wait("ve", "sc", b.cnt["sc"])
    for c in range(NCH):
        b.op("ve", lambda E, c=c: E.tensor_reduce(
            STAT[:, c:c + 1], SG1[:, c * 8:c * 8 + 8], AX.X, ALU.add))
    for c in range(NCH):
        b.op("ve", lambda E, c=c: E.tensor_reduce(
            STAT[:, 16 + c:17 + c], SG2[:, c * 8:c * 8 + 8], AX.X, ALU.add))
    b.op("ve", lambda E: E.tensor_scalar(VJ[:], VJ[:], 1.0, None, ALU.mult), inc="ve")
    v_fs = stats_finalize(HF)
    XTF = [PF3[:, 2048:3072], PF3[:, 3072:4096]]
    xt_free = [("ve", 0), ("ve", 0)]
    v_fin = 0
    for c in range(NCH):
        pb = c % 2
        q = "qx0" if pb == 0 else "qx1"
        b.wait("sy", *xt_free[pb])
        cq = b.dma("sy", q, XTF[pb][:], X[c * 128:(c + 1) * 128, :])
        b.wait("ve", q, cq)
        b.wait("ve", "ve", max(v_fs, v_bb))
        b.op("ve", lambda E, c=c, pb=pb: E.scalar_tensor_tensor(
            accs(c), accs(c), STAT[:, 48 + c:49 + c], XTF[pb][:], ALU.mult, ALU.add))
        b.op("ve", lambda E, c=c: E.scalar_tensor_tensor(
            accs(c), B1B, STAT[:, c:c + 1], accs(c), ALU.mult, ALU.add))
        v_fin = b.op("ve", lambda E, c=c: E.tensor_tensor(
            accs(c), accs(c), B2B, ALU.add), inc="ve")
        xt_free[pb] = ("ve", v_fin)

    if PHASES != "ffn0":
        WHq2 = [PBF[:, 0:4096], PBF[:, 8192:12288]]
        WLq2 = [PBF[:, 4096:8192], PBF[:, 12288:16384]]
        QHT = [PBF[:, 16384:16896], PBF[:, 17408:17920]]
        QLT = [PBF[:, 16896:17408], PBF[:, 17920:18432]]
        STGQ = [PBF[:, 18432:19456], PBF[:, 19456:20480]]
        W32S = [PF3[:, 0:512], PF3[:, 512:1024]]
        TBLC = [PF3[:, 1024:1536], PF3[:, 4096:4608]]
        TBLS = [PF3[:, 1536:2048], PF3[:, 4608:5120]]
        SWP = [PF3[:, 5120:5632], PF3[:, 5632:6144]]
        QXM = [PF3[:, 2048:2560], PF3[:, 2560:3072]]

        wh_free2 = [("te", 0), ("te", 0)]
        w32_free = [None, None]
        qt_free = [[], []]
        stg_free = [[], []]
        ph_free2 = [[], []]
        pt_free3 = [("ve", 0), ("ve", 0)]
        tbl_free = [[("ve", v_fin)], [("ve", v_fin)]]

        def qk_tail(pb, kind, ot, c, v_sp):
            b.wait("te", "ve", v_sp)
            b.wait("te", *pt_free3[pb])
            for k in range(4):
                b.op("te", lambda E, k=k, pb=pb: E.transpose(
                    PT[:, pb * 1024 + k * 128:pb * 1024 + (k + 1) * 128],
                    QHT[pb][:, k * 128:(k + 1) * 128], IDE[:]))
            for k in range(4):
                kw = {"inc": "te"} if k == 3 else {}
                b.op("te", lambda E, k=k, pb=pb: E.transpose(
                    PT[:, pb * 1024 + 512 + k * 128:pb * 1024 + 512 + (k + 1) * 128],
                    QLT[pb][:, k * 128:(k + 1) * 128], IDE[:]), **kw)
            t_tr = b.cnt["te"]
            qt_free[pb] = [("te", t_tr)]
            b.wait("ve", "te", t_tr)
            for w in stg_free[pb]:
                b.wait("ve", *w)
            v_st = b.op("ve", lambda E, pb=pb: E.tensor_copy(
                STGQ[pb][:], PT[:, pb * 1024:(pb + 1) * 1024]), inc="ve")
            pt_free3[pb] = ("ve", v_st)
            b.wait("gp", "ve", v_st)
            qs = "qsp" if pb == 0 else "qsp1"
            DH, DL = (QFH, QFL) if kind == "q" else (KFH, KFL)
            b.dma("gp", qs,
                  DH[ot * 4:(ot + 1) * 4, :, c * 128:(c + 1) * 128].rearrange("k p t -> p k t"),
                  STGQ[pb][:, 0:512].rearrange("p (k t) -> p k t", k=4))
            cs = b.dma("gp", qs,
                       DL[ot * 4:(ot + 1) * 4, :, c * 128:(c + 1) * 128].rearrange("k p t -> p k t"),
                       STGQ[pb][:, 512:1024].rearrange("p (k t) -> p k t", k=4))
            stg_free[pb] = [(qs, cs)]

        qk_pending = None
        slices = [(mi, ot) for mi in range(3) for ot in range(2)]
        wsrcs = (WQ, WK0, WV0)
        kinds = ("q", "k", "v")
        slice_info = {}

        def qkv_load_slice(si):
            mi, ot = slices[si]
            sb_ = si % 2
            WSRC = wsrcs[mi]
            b.wait("gp", *wh_free2[sb_])
            if si < 2:
                b.wait("gp", *wf_free[0])
                b.wait("gp", *wf_free[1])
                b.wait("gp", *wp_free[0])
                b.wait("gp", *wp_free[1])
                b.wait("gp", *gh_free[0])
                b.wait("gp", *gh_free[1])
                b.wait("gp", *gf_free[0])
                b.wait("gp", *gf_free[1])
            ch = b.dma("gp", "qsp", WHq2[sb_][:].rearrange("p (e o) -> p e o", e=DCH),
                       WSRC[:, ot * 512:(ot + 1) * 512].rearrange("(e p) o -> p e o", e=DCH))
            v_lo = 0
            for qtr in range(8):
                pq = qtr % 2
                qq = "qw0" if pq == 0 else "qw1"
                if w32_free[pq]:
                    b.wait("sy", *w32_free[pq])
                b.wait("sy", *wh_free2[sb_])
                cq0 = b.dma("sy", qq, W32S[pq][:],
                            WSRC[qtr * 128:(qtr + 1) * 128, ot * 512:(ot + 1) * 512])
                b.wait("ve", qq, cq0)
                b.wait("ve", "qsp", ch)
                v_lo = b.op("ve", lambda E, pq=pq, qtr=qtr, sb_=sb_: E.tensor_tensor(
                    WLq2[sb_][:, qtr * 512:(qtr + 1) * 512], W32S[pq][:],
                    WHq2[sb_][:, qtr * 512:(qtr + 1) * 512], ALU.subtract), inc="ve")
                w32_free[pq] = ("ve", v_lo)
            slice_info[si] = (ch, v_lo)

        qkv_load_slice(0)
        for si in range(6):
            mi, ot = slices[si]
            kind = kinds[mi]
            WSRC = wsrcs[mi]
            sb_ = si % 2
            ch, v_lo = slice_info[si]
            WHq = WHq2[sb_]
            WLq = WLq2[sb_]
            if si + 1 < 6:
                qkv_load_slice(si + 1)
            if True:
                for c in range(NCH):
                    pb = c % 2
                    ct = None
                    if kind in ("q", "k"):
                        TC, TS = (TQC, TQS) if kind == "q" else (TKC, TKS)
                        qq = "qx0" if pb == 0 else "qx1"
                        for w in tbl_free[pb]:
                            b.wait("sy", *w)
                        b.dma("sy", qq, TBLC[pb][:], TC[c * 128:(c + 1) * 128, ot * 512:(ot + 1) * 512])
                        ct = b.dma("sy", qq, TBLS[pb][:], TS[c * 128:(c + 1) * 128, ot * 512:(ot + 1) * 512])
                    b.wait("te", "qsp", ch)
                    b.wait("te", "ve", v_lo)
                    for w in ph_free2[pb]:
                        b.wait("te", *w)
                    n_mm = 0
                    for e in range(DCH):
                        for XS, WS in ((XNH, WHq), (XNL, WHq), (XNH, WLq)):
                            n_mm += 1
                            kw = {"inc": "te"} if n_mm == 24 else {}
                            b.op("te", lambda E, e=e, c=c, pb=pb, XS=XS, WS=WS,
                                 st=(n_mm == 1), sp=(n_mm == 24): E.matmul(
                                PH[pb][:], XS[:, e * T + c * 128:e * T + (c + 1) * 128],
                                WS[:, e * 512:(e + 1) * 512], start=st, stop=sp), **kw)
                    t_p = b.cnt["te"]
                    wh_free2[sb_] = ("te", t_p)
                    if qk_pending is not None:
                        qk_tail(*qk_pending)
                        qk_pending = None
                    b.wait("ve", "te", t_p)
                    for w in qt_free[pb]:
                        b.wait("ve", *w)
                    if kind == "v":
                        b.op("ve", lambda E, pb=pb: E.tensor_copy(QHT[pb][:], PH[pb][:]))
                        v_ev = b.op("ve", lambda E, pb=pb: E.tensor_tensor(
                            QLT[pb][:], PH[pb][:], QHT[pb][:], ALU.subtract), inc="ve")
                        ph_free2[pb] = [("ve", v_ev)]
                        b.wait("gp", "ve", v_ev)
                        qs = "qsp" if pb == 0 else "qsp1"
                        b.dma("gp", qs, VTH[c, :, ot * 512:(ot + 1) * 512], QHT[pb][:])
                        cs = b.dma("gp", qs, VTL[c, :, ot * 512:(ot + 1) * 512], QLT[pb][:])
                        qt_free[pb] = [(qs, cs)]
                    else:
                        b.wait("ve", qq, ct)
                        b.op("ve", lambda E, pb=pb: E.tensor_copy(
                            SWP[pb][:].rearrange("p (o k) -> p o k", k=2)[:, :, 0:1],
                            PH[pb][:].rearrange("p (o k) -> p o k", k=2)[:, :, 1:2]))
                        b.op("ve", lambda E, pb=pb: E.tensor_copy(
                            SWP[pb][:].rearrange("p (o k) -> p o k", k=2)[:, :, 1:2],
                            PH[pb][:].rearrange("p (o k) -> p o k", k=2)[:, :, 0:1]))
                        b.op("ve", lambda E, pb=pb: E.tensor_tensor(
                            QXM[pb][:], PH[pb][:], TBLC[pb][:], ALU.mult))
                        v_ph = b.op("ve", lambda E, pb=pb: E.tensor_tensor(
                            SWP[pb][:], SWP[pb][:], TBLS[pb][:], ALU.mult), inc="ve")
                        ph_free2[pb] = [("ve", v_ph)]
                        b.op("ve", lambda E, pb=pb: E.tensor_tensor(
                            QXM[pb][:], QXM[pb][:], SWP[pb][:], ALU.add))
                        b.op("ve", lambda E, pb=pb: E.tensor_copy(QHT[pb][:], QXM[pb][:]))
                        v_sp = b.op("ve", lambda E, pb=pb: E.tensor_tensor(
                            QLT[pb][:], QXM[pb][:], QHT[pb][:], ALU.subtract), inc="ve")
                        tbl_free[pb] = [("ve", v_sp)]
                        qk_pending = (pb, kind, ot, c, v_sp)

        if qk_pending is not None:
            qk_tail(*qk_pending)
            qk_pending = None

    if PHASES != "ffn0":
        b.barrier()
        QFB_H = PBF[:, 0:2048]
        QFB_L = PBF[:, 2048:4096]
        KFB_H = [PBF[:, 4096:5120], PBF[:, 6144:7168]]
        KFB_L = [PBF[:, 5120:6144], PBF[:, 7168:8192]]
        VFB_H = [PBF[:, 8192:9216], PBF[:, 10240:11264]]
        VFB_L = [PBF[:, 9216:10240], PBF[:, 11264:12288]]
        SB_H = PBF[:, 16384:17408]
        SB_L = PBF[:, 17408:18432]
        MASKT = PBF[:, 18432:18944]
        SMT = PF3[:, 0:512]

        cm = b.dma("sy", "qw0", MASKT[:].rearrange("p (r q) -> p r q", r=2),
                   MASKSI.rearrange("(r p) q -> p r q", r=4)[:, 0:2, 0:256])
        qf_free = ("te", 0)
        kf_free = [[], []]
        vf_free = [[], []]
        sb_free = ("te", 0)
        ph_free3 = [[], []]
        for j8 in range(8):
            b.wait("sy", *qf_free)
            b.dma("sy", "qw0", QFB_H[:].rearrange("p (e t) -> p e t", e=DCH),
                  QFH[:, :, j8 * 256:(j8 + 1) * 256].rearrange("e p t -> p e t"))
            cq1 = b.dma("sy", "qw0", QFB_L[:].rearrange("p (e t) -> p e t", e=DCH),
                        QFL[:, :, j8 * 256:(j8 + 1) * 256].rearrange("e p t -> p e t"))
            n_mc_tot = 2 * j8 + 2
            nmq = (n_mc_tot + 3) // 4
            for mq in range(nmq):
                mcs = list(range(mq * 4, min(mq * 4 + 4, n_mc_tot)))
                for mi_, mc in enumerate(mcs):
                    pk = mc % 2
                    qk = "qx0" if pk == 0 else "qx1"
                    for w in kf_free[pk]:
                        b.wait("sy", *w)
                    b.dma("sy", qk, KFB_H[pk][:].rearrange("p (e t) -> p e t", e=DCH),
                          KFH[:, :, mc * 128:(mc + 1) * 128].rearrange("e p t -> p e t"))
                    ck = b.dma("sy", qk, KFB_L[pk][:].rearrange("p (e t) -> p e t", e=DCH),
                               KFL[:, :, mc * 128:(mc + 1) * 128].rearrange("e p t -> p e t"))
                    b.wait("te", qk, ck)
                    b.wait("te", "qw0", cq1)
                    for w in ph_free3[pk]:
                        b.wait("te", *w)
                    n_mm = 0
                    for e in range(DCH):
                        for KS, QS in ((KFB_H[pk], QFB_H), (KFB_H[pk], QFB_L), (KFB_L[pk], QFB_H)):
                            n_mm += 1
                            kw = {"inc": "te"} if n_mm == 24 else {}
                            b.op("te", lambda E, e=e, pk=pk, KS=KS, QS=QS,
                                 st=(n_mm == 1), sp=(n_mm == 24): E.matmul(
                                PH[pk][:, 0:256], KS[:, e * 128:(e + 1) * 128],
                                QS[:, e * 256:(e + 1) * 256], start=st, stop=sp), **kw)
                    t_s = b.cnt["te"]
                    kf_free[pk] = [("te", t_s)]
                    b.wait("ve", "te", t_s)
                    b.wait("ve", *sb_free)
                    b.wait("ve", "qw0", cm)
                    r = mc - 2 * j8
                    sl = slice(mi_ * 256, (mi_ + 1) * 256)
                    if r >= 0:
                        b.op("ve", lambda E, pk=pk, r=r: E.tensor_tensor(
                            SMT[:, 0:256], PH[pk][:, 0:256], MASKT[:, r * 256:(r + 1) * 256], ALU.mult))
                        b.op("ve", lambda E, sl=sl: E.tensor_copy(SB_H[:, sl], SMT[:, 0:256]))
                        v_se = b.op("ve", lambda E, sl=sl: E.tensor_tensor(
                            SB_L[:, sl], SMT[:, 0:256], SB_H[:, sl], ALU.subtract), inc="ve")
                    else:
                        b.op("ve", lambda E, pk=pk, sl=sl: E.tensor_copy(SB_H[:, sl], PH[pk][:, 0:256]))
                        v_se = b.op("ve", lambda E, pk=pk, sl=sl: E.tensor_tensor(
                            SB_L[:, sl], PH[pk][:, 0:256], SB_H[:, sl], ALU.subtract), inc="ve")
                    ph_free3[pk] = [("ve", v_se)]
                b.wait("te", "ve", b.cnt["ve"])
                for mi_, mc in enumerate(mcs):
                    pv = mc % 2
                    qv = "qx0" if pv == 0 else "qx1"
                    for w in vf_free[pv]:
                        b.wait("sy", *w)
                    b.dma("sy", qv, VFB_H[pv][:], VTH[mc])
                    cv = b.dma("sy", qv, VFB_L[pv][:], VTL[mc])
                    b.wait("te", qv, cv)
                    for oh in range(2):
                        for ns in range(2):
                            for pi, (SS, VS) in enumerate(
                                    ((SB_H, VFB_H[pv]), (SB_L, VFB_H[pv]), (SB_H, VFB_L[pv]))):
                                first = (mq == 0 and mi_ == 0 and pi == 0)
                                last = (mq == nmq - 1 and mi_ == len(mcs) - 1 and pi == 2)
                                kw = {"inc": "te"} if (last and oh == 1 and ns == 1) else {}
                                b.op("te", lambda E, mi_=mi_, oh=oh, ns=ns, SS=SS, VS=VS,
                                     st=first, sp=last: E.matmul(
                                    PY[ns][:, oh * 512:(oh + 1) * 512],
                                    SS[:, mi_ * 256 + ns * 128:mi_ * 256 + (ns + 1) * 128],
                                    VS[:, oh * 512:(oh + 1) * 512], start=st, stop=sp), **kw)
                    vf_free[pv] = [("te", b.cnt["te"])]
                sb_free = ("te", b.cnt["te"])
            t_out = b.cnt["te"]
            qf_free = ("te", t_out)
            b.wait("ve", "te", t_out)
            for ns in range(2):
                c = 2 * j8 + ns
                kw = {"inc": "ve"} if ns == 1 else {}
                b.op("ve", lambda E, ns=ns, c=c: E.tensor_tensor(
                    accs(c), accs(c), PY[ns][:], ALU.add), **kw)
            b.wait("te", "ve", b.cnt["ve"])

    if PHASES not in ("ffn0", "l0"):
        b.barrier()
        v_stats2 = ln_stats(False)
        v_xn2 = ln_passB(False, v_stats2)
        b.barrier()

        LG = PF3[:, 0:128]
        EA = PF3[:, 128:256]
        EB = PF3[:, 256:384]
        WG32 = PF3[:, 384:448]
        M1 = GTH[:, 0:16]
        M2 = STAT[:, 0:16]
        DEN = STAT[:, 16:32]
        WGB_H = PBF[:, 0:64]
        WGB_L = PBF[:, 64:128]
        cg = b.dma("sy", "qw0", WG32[:].rearrange("p (e g) -> p e g", e=DCH),
                   WG.rearrange("(e p) g -> p e g", e=DCH))
        b.wait("ve", "qw0", cg)
        b.op("ve", lambda E: E.tensor_copy(WGB_H[:], WG32[:]))
        v_wg = b.op("ve", lambda E: E.tensor_tensor(WGB_L[:], WG32[:], WGB_H[:], ALU.subtract), inc="ve")
        b.wait("te", "ve", max(v_wg, v_xn2))
        lg_free = [[], []]
        for c in range(NCH):
            pb = c % 2
            for w in lg_free[pb]:
                b.wait("te", *w)
            n_mm = 0
            for e in range(DCH):
                for XS, WS in ((XNH, WGB_H), (XNL, WGB_H), (XNH, WGB_L)):
                    n_mm += 1
                    kw = {"inc": "te"} if n_mm == 24 else {}
                    b.op("te", lambda E, e=e, c=c, pb=pb, XS=XS, WS=WS,
                         st=(n_mm == 1), sp=(n_mm == 24): E.matmul(
                        PH[pb][:, 0:8], XS[:, e * T + c * 128:e * T + (c + 1) * 128],
                        WS[:, e * 8:(e + 1) * 8], start=st, stop=sp), **kw)
            t_g = b.cnt["te"]
            b.wait("ve", "te", t_g)
            v_c = b.op("ve", lambda E, c=c, pb=pb: E.tensor_copy(
                LG[:, c * 8:(c + 1) * 8], PH[pb][:, 0:8]), inc="ve")
            lg_free[pb] = [("ve", v_c)]
        LG3 = LG.rearrange("p (c g) -> p c g", g=NEXP)
        EA3 = EA.rearrange("p (c g) -> p c g", g=NEXP)
        EB3 = EB.rearrange("p (c g) -> p c g", g=NEXP)
        M1u = M1.rearrange("p (c u) -> p c u", u=1).broadcast_to([128, NCH, NEXP])
        M2u = M2.rearrange("p (c u) -> p c u", u=1).broadcast_to([128, NCH, NEXP])
        DENu = DEN.rearrange("p (c u) -> p c u", u=1).broadcast_to([128, NCH, NEXP])
        vgap()
        b.op("ve", lambda E: E.tensor_reduce(
            M1.rearrange("p (c u) -> p c u", u=1), LG3, AX.X, ALU.max))
        vgap()
        v1 = b.op("ve", lambda E: E.tensor_tensor(EA3, LG3, M1u, ALU.subtract), inc="ve")
        b.wait("sc", "ve", v1)
        s_e = b.op("sc", lambda E: E.activation(EA[:], EA[:], AF.Exp), inc="sc")
        b.op("ve", lambda E: E.tensor_tensor(EB3, LG3, M1u, ALU.is_ge))
        vgap()
        b.op("ve", lambda E: E.tensor_scalar(EB[:], EB[:], 1e30, None, ALU.mult))
        vgap()
        b.op("ve", lambda E: E.tensor_tensor(EB[:], LG[:], EB[:], ALU.subtract))
        vgap()
        b.op("ve", lambda E: E.tensor_reduce(
            M2.rearrange("p (c u) -> p c u", u=1), EB3, AX.X, ALU.max))
        vgap()
        b.op("ve", lambda E: E.tensor_tensor(EB3, LG3, M2u, ALU.is_ge))
        b.wait("ve", "sc", s_e)
        vgap()
        b.op("ve", lambda E: E.tensor_tensor(EA[:], EA[:], EB[:], ALU.mult))
        vgap()
        b.op("ve", lambda E: E.tensor_reduce(
            DEN.rearrange("p (c u) -> p c u", u=1), EA3, AX.X, ALU.add))
        vgap()
        b.op("ve", lambda E: E.reciprocal(DEN[:], DEN[:]))
        vgap()
        v_cgt = b.op("ve", lambda E: E.tensor_tensor(
            CGT[:].rearrange("p (c g) -> p c g", g=NEXP), EA3, DENu, ALU.mult), inc="ve")

        b.barrier()
        W1S = PF3[:, 0:1024]
        WKS = [PF3[:, 1024:1536], PF3[:, 1536:2048]]
        AFB = PF3[:, 2048:2560]
        W1H = PBF[:, 128:1152]
        W1L = PBF[:, 1152:2176]
        W1TH = PBF[:, 2176:2688]
        W1TL = PBF[:, 2688:3200]
        MTH = PBF[:, 3200:3712]
        MTL = PBF[:, 3712:4224]
        W2TH = PBF[:, 4224:5248]
        W2TL = PBF[:, 5248:6272]
        WKH = [PBF[:, 6272:6784], PBF[:, 7296:7808]]
        WKL = [PBF[:, 6784:7296], PBF[:, 7808:8320]]
        MHB = PBF[:, 8320:9344]
        MLB = PBF[:, 9344:10368]
        AFH = PBF[:, 10368:10880]
        AFL = PBF[:, 10880:11392]

        cw1 = b.dma("sy", "qw0", W1S[0:64, :], W1A)
        b.wait("ve", "qw0", cw1)
        b.op("ve", lambda E: E.tensor_copy(W1H[0:64, :], W1S[0:64, :]))
        v_w1 = b.op("ve", lambda E: E.tensor_tensor(
            W1L[0:64, :], W1S[0:64, :], W1H[0:64, :], ALU.subtract), inc="ve")
        b.wait("te", "ve", v_w1)
        for e in range(DCH):
            b.op("te", lambda E, e=e: E.transpose(
                PT[:, e * 64:(e + 1) * 64], W1H[0:64, e * 128:(e + 1) * 128], IDE[0:64, 0:64]))
        for e in range(DCH):
            kw = {"inc": "te"} if e == DCH - 1 else {}
            b.op("te", lambda E, e=e: E.transpose(
                PT[:, 512 + e * 64:512 + (e + 1) * 64],
                W1L[0:64, e * 128:(e + 1) * 128], IDE[0:64, 0:64]), **kw)
        t_w1t = b.cnt["te"]
        b.wait("ve", "te", t_w1t)
        b.op("ve", lambda E: E.tensor_copy(W1TH[:], PT[:, 0:512]))
        v_w1t = b.op("ve", lambda E: E.tensor_copy(W1TL[:], PT[:, 512:1024]), inc="ve")
        wks_free = [[], []]
        for oh in range(2):
            for kc in range(DCH):
                pk = kc % 2
                qq = "qx0" if pk == 0 else "qx1"
                for w in wks_free[pk]:
                    b.wait("sy", *w)
                ckk = b.dma("sy", qq, WKS[pk][:], WK1[kc * 128:(kc + 1) * 128, oh * 512:(oh + 1) * 512])
                b.wait("ve", qq, ckk)
                b.op("ve", lambda E, pk=pk: E.tensor_copy(WKH[pk][:], WKS[pk][:]))
                v_wk = b.op("ve", lambda E, pk=pk: E.tensor_tensor(
                    WKL[pk][:], WKS[pk][:], WKH[pk][:], ALU.subtract), inc="ve")
                b.wait("te", "ve", max(v_wk, v_w1t))
                n_mm = 0
                for W1T_, WKX in ((W1TH, WKH[pk]), (W1TL, WKH[pk]), (W1TH, WKL[pk])):
                    n_mm += 1
                    kw = {"inc": "te"} if (kc == DCH - 1 and n_mm == 3) else {}
                    b.op("te", lambda E, kc=kc, oh=oh, W1T_=W1T_, WKX=WKX,
                         st=(kc == 0 and n_mm == 1), sp=(kc == DCH - 1 and n_mm == 3): E.matmul(
                        PY[oh][0:64, 0:512], W1T_[:, kc * 64:(kc + 1) * 64],
                        WKX[:], start=st, stop=sp), **kw)
                wks_free[pk] = [("te", b.cnt["te"]), ("ve", v_wk)]
        t_m = b.cnt["te"]
        b.wait("ve", "te", t_m)
        b.op("ve", lambda E: E.tensor_copy(MHB[0:64, 0:512], PY[0][0:64, 0:512]))
        b.op("ve", lambda E: E.tensor_copy(MHB[0:64, 512:1024], PY[1][0:64, 0:512]))
        b.op("ve", lambda E: E.tensor_tensor(
            MLB[0:64, 0:512], PY[0][0:64, 0:512], MHB[0:64, 0:512], ALU.subtract))
        v_m = b.op("ve", lambda E: E.tensor_tensor(
            MLB[0:64, 512:1024], PY[1][0:64, 0:512], MHB[0:64, 512:1024], ALU.subtract), inc="ve")
        b.wait("te", "ve", v_m)
        for e in range(DCH):
            b.op("te", lambda E, e=e: E.transpose(
                PT[:, 1024 + e * 64:1024 + (e + 1) * 64], MHB[0:64, e * 128:(e + 1) * 128], IDE[0:64, 0:64]))
        for e in range(DCH):
            kw = {"inc": "te"} if e == DCH - 1 else {}
            b.op("te", lambda E, e=e: E.transpose(
                PT[:, 1536 + e * 64:1536 + (e + 1) * 64], MLB[0:64, e * 128:(e + 1) * 128], IDE[0:64, 0:64]), **kw)
        t_mt = b.cnt["te"]
        b.wait("ve", "te", t_mt)
        b.op("ve", lambda E: E.tensor_copy(MTH[:], PT[:, 1024:1536]))
        v_mt = b.op("ve", lambda E: E.tensor_copy(MTL[:], PT[:, 1536:2048]), inc="ve")
        w2_parts = []
        for hb in range(DCH):
            pk = hb % 2
            qq = "qx0" if pk == 0 else "qx1"
            for w in wks_free[pk]:
                b.wait("sy", *w)
            ck2 = b.dma("sy", qq, WKS[pk][:, 0:64], W2A[hb * 128:(hb + 1) * 128, :])
            b.wait("ve", qq, ck2)
            b.wait("ve", "te", t_mt)
            b.op("ve", lambda E, pk=pk: E.tensor_copy(WKH[pk][:, 0:64], WKS[pk][:, 0:64]))
            v_w2 = b.op("ve", lambda E, pk=pk: E.tensor_tensor(
                WKL[pk][:, 0:64], WKS[pk][:, 0:64], WKH[pk][:, 0:64], ALU.subtract), inc="ve")
            b.wait("te", "ve", max(v_w2, v_mt))
            b.op("te", lambda E, pk=pk, hb=hb: E.transpose(
                PT[0:64, hb * 128:(hb + 1) * 128], WKH[pk][:, 0:64], IDE[:]))
            kw = {"inc": "te"} if hb == DCH - 1 else {}
            b.op("te", lambda E, pk=pk, hb=hb: E.transpose(
                PT[0:64, 1024 + hb * 128:1024 + (hb + 1) * 128], WKL[pk][:, 0:64], IDE[:]), **kw)
            wks_free[pk] = [("te", b.cnt["te"]), ("ve", v_w2)]
        t_w2t = b.cnt["te"]
        b.wait("ve", "te", t_w2t)
        b.op("ve", lambda E: E.tensor_copy(W2TH[0:64, :], PT[0:64, 0:1024]))
        v_w2t = b.op("ve", lambda E: E.tensor_copy(W2TL[0:64, :], PT[0:64, 1024:2048]), inc="ve")
        b.wait("te", "ve", max(v_mt, v_w2t))
        af_free = []
        afb_free = []
        ry_free = [[], []]
        for tt in range(4):
            pb = tt % 2
            n_mm = 0
            for e in range(DCH):
                for MT_, XS in ((MTH, XNH), (MTH, XNL), (MTL, XNH)):
                    n_mm += 1
                    kw = {"inc": "te"} if n_mm == 24 else {}
                    b.op("te", lambda E, e=e, tt=tt, pb=pb, MT_=MT_, XS=XS,
                         st=(n_mm == 1), sp=(n_mm == 24): E.matmul(
                        PH[pb][0:64, 0:512], MT_[:, e * 64:(e + 1) * 64],
                        XS[:, e * T + tt * 512:e * T + (tt + 1) * 512], start=st, stop=sp), **kw)
            t_a = b.cnt["te"]
            b.wait("sc", "te", t_a)
            for w in afb_free:
                b.wait("sc", *w)
            s_a = b.op("sc", lambda E, pb=pb: E.activation(
                AFB[0:64, :], PH[pb][0:64, 0:512], AF.Gelu), inc="sc")
            b.wait("ve", "sc", s_a)
            for w in af_free:
                b.wait("ve", *w)
            b.op("ve", lambda E: E.tensor_copy(AFH[0:64, :], AFB[0:64, :]))
            v_af = b.op("ve", lambda E: E.tensor_tensor(
                AFL[0:64, :], AFB[0:64, :], AFH[0:64, :], ALU.subtract), inc="ve")
            b.wait("te", "ve", v_af)
            for idx, (ns, oh) in enumerate([(n_, o_) for n_ in range(4) for o_ in range(2)]):
                pr = idx % 2
                for w in ry_free[pr]:
                    b.wait("te", *w)
                n_mm = 0
                for AS, WS in ((AFH, W2TH), (AFL, W2TH), (AFH, W2TL)):
                    n_mm += 1
                    kw = {"inc": "te"} if n_mm == 3 else {}
                    b.op("te", lambda E, ns=ns, oh=oh, pr=pr, AS=AS, WS=WS,
                         st=(n_mm == 1), sp=(n_mm == 3): E.matmul(
                        PY[pr][:, 0:512], AS[0:64, ns * 128:(ns + 1) * 128],
                        WS[0:64, oh * 512:(oh + 1) * 512], start=st, stop=sp), **kw)
                t_r = b.cnt["te"]
                b.wait("ve", "te", t_r)
                c = tt * 4 + ns
                v_r = b.op("ve", lambda E, c=c, oh=oh, pr=pr: E.tensor_tensor(
                    accs(c, oh * 512, 512), accs(c, oh * 512, 512), PY[pr][:, 0:512], ALU.add), inc="ve")
                ry_free[pr] = [("ve", v_r)]
            af_free = [("te", b.cnt["te"])]
            afb_free = [("ve", v_af)]

        moe_on = PHASES != "reads"
        b.barrier()
        W1BH = PBF[:, 0:2048]
        W1BL = PBF[:, 2048:4096]
        W2BH = PBF[:, 4096:6144]
        W2BL = PBF[:, 6144:8192]
        HFB_H = PBF[:, 8192:12288]
        HFB_L = PBF[:, 12288:16384]
        WST = PF3[:, 0:2048]
        w1_free = []
        w2_free = []
        wst_free = []
        hf_free = []
        ph_free4 = [[], []]
        py_free4 = [[], []]
        for ex in range(NEXP if moe_on else 0):
            for g in range(8):
                for w in w1_free:
                    b.wait("gp", *w)
                ch1 = b.dma("gp", "qsp", W1BH[:].rearrange("p (e f) -> p e f", e=DCH),
                            WE1[ex, :, g * 256:(g + 1) * 256].rearrange("(e p) f -> p e f", e=DCH))
                for w in wst_free:
                    b.wait("sy", *w)
                for w in w1_free:
                    b.wait("sy", *w)
                cq1 = b.dma("sy", "qw0", WST[:].rearrange("p (e f) -> p e f", e=DCH),
                            WE1[ex, :, g * 256:(g + 1) * 256].rearrange("(e p) f -> p e f", e=DCH))
                b.wait("ve", "qw0", cq1)
                b.wait("ve", "qsp", ch1)
                for w in w1_free:
                    b.wait("ve", *w)
                v_l1 = b.op("ve", lambda E: E.tensor_tensor(
                    W1BL[:], WST[:], W1BH[:], ALU.subtract), inc="ve")
                wst_free = [("ve", v_l1)]
                for fc in range(2):
                    for tt in range(4):
                        pb = (fc * 4 + tt) % 2
                        for w in ph_free4[pb]:
                            b.wait("te", *w)
                        b.wait("te", "ve", v_l1)
                        for w in hf_free:
                            b.wait("te", *w)
                        n_mm = 0
                        for e in range(DCH):
                            for WS, XS in ((W1BH, XNH), (W1BH, XNL), (W1BL, XNH)):
                                n_mm += 1
                                kw = {"inc": "te"} if n_mm == 24 else {}
                                b.op("te", lambda E, e=e, fc=fc, tt=tt, pb=pb, WS=WS, XS=XS,
                                     st=(n_mm == 1), sp=(n_mm == 24): E.matmul(
                                    PH[pb][:], WS[:, e * 256 + fc * 128:e * 256 + (fc + 1) * 128],
                                    XS[:, e * T + tt * 512:e * T + (tt + 1) * 512],
                                    start=st, stop=sp), **kw)
                        t_h = b.cnt["te"]
                        b.wait("ve", "te", t_h)
                        off = fc * 2048 + tt * 512
                        b.op("ve", lambda E, pb=pb, off=off: E.tensor_scalar(
                            HFB_H[:, off:off + 512], PH[pb][:], 0.0, None, ALU.max))
                        v_hf = b.op("ve", lambda E, pb=pb, off=off: E.scalar_tensor_tensor(
                            HFB_L[:, off:off + 512], PH[pb][:], 0.0,
                            HFB_H[:, off:off + 512], ALU.max, ALU.subtract), inc="ve")
                        ph_free4[pb] = [("ve", v_hf)]
                v_hall = b.cnt["ve"]
                for w in w2_free:
                    b.wait("gp", *w)
                ch2 = b.dma("gp", "qsp1", W2BH[:].rearrange("p (e o) -> p e o", e=2),
                            WE2[ex, g * 256:(g + 1) * 256, :].rearrange("(e p) o -> p e o", e=2))
                for w in wst_free:
                    b.wait("sy", *w)
                for w in w2_free:
                    b.wait("sy", *w)
                cq2 = b.dma("sy", "qw1", WST[:].rearrange("p (e o) -> p e o", e=2),
                            WE2[ex, g * 256:(g + 1) * 256, :].rearrange("(e p) o -> p e o", e=2))
                b.wait("ve", "qw1", cq2)
                b.wait("ve", "qsp1", ch2)
                for w in w2_free:
                    b.wait("ve", *w)
                v_l2 = b.op("ve", lambda E: E.tensor_tensor(
                    W2BL[:], WST[:], W2BH[:], ALU.subtract), inc="ve")
                wst_free = [("ve", v_l2)]
                b.wait("te", "ve", max(v_l2, v_hall, v_cgt))
                for c in range(NCH):
                    pb = c % 2
                    for w in py_free4[pb]:
                        b.wait("te", *w)
                    n_mm = 0
                    for fc in range(2):
                        for HS, WS in ((HFB_H, W2BH), (HFB_L, W2BH), (HFB_H, W2BL)):
                            for oh in range(2):
                                n_mm += 1
                                kw = {"inc": "te"} if n_mm == 12 else {}
                                st_ = (fc == 0 and HS is HFB_H and WS is W2BH)
                                sp_ = (fc == 1 and HS is HFB_H and WS is W2BL)
                                b.op("te", lambda E, c=c, pb=pb, fc=fc, oh=oh, HS=HS, WS=WS,
                                     st=st_, sp=sp_: E.matmul(
                                    PY[pb][:, oh * 512:(oh + 1) * 512],
                                    HS[:, fc * 2048 + (c // 4) * 512 + (c % 4) * 128:
                                       fc * 2048 + (c // 4) * 512 + (c % 4 + 1) * 128],
                                    WS[:, fc * 1024 + oh * 512:fc * 1024 + (oh + 1) * 512],
                                    start=st, stop=sp), **kw)
                    t_y = b.cnt["te"]
                    b.wait("ve", "te", t_y)
                    v_y = b.op("ve", lambda E, c=c, pb=pb, ex=ex: E.scalar_tensor_tensor(
                        accs(c), PY[pb][:], CGT[:, c * NEXP + ex:c * NEXP + ex + 1],
                        accs(c), ALU.mult, ALU.add), inc="ve")
                    py_free4[pb] = [("ve", v_y)]
                hf_free = [("te", b.cnt["te"])]
                w1_free = [("te", t_h)]
                w2_free = [("te", b.cnt["te"])]

    b.wait("sy", "ve", b.cnt["ve"])
    co = b.dma("sy", "qo", Y.rearrange("(c p) d -> p c d", p=128),
               ACC[:].rearrange("p (c d) -> p c d", c=NCH))
    b.wait("sy", "qo", co)

    with ExitStack() as outer:
        for s in b.sem_names:
            b.sems[s] = outer.enter_context(nc.semaphore(s))
        blk = outer.enter_context(nc.Block())

        @blk.sync
        def _(E):
            for fn in b.ops["sy"]:
                fn(E)

        @blk.gpsimd
        def _(E):
            for fn in b.ops["gp"]:
                fn(E)

        @blk.vector
        def _(E):
            for fn in b.ops["ve"]:
                fn(E)

        @blk.scalar
        def _(E):
            for fn in b.ops["sc"]:
                fn(E)

        @blk.tensor
        def _(E):
            for fn in b.ops["te"]:
                fn(E)
    es.close()
    return nc


_CST = None


def kernel(**inputs):
    global _CST
    if _CST is None:
        _CST = _host_constants()
    cst = _CST
    nc = bass.Bass("TRN2", target_bir_lowering=False, detect_race_conditions=False)
    build(nc)
    xs = np.ascontiguousarray(inputs["L0_attn_X"], dtype=np.float32)
    base = {
        "w_q": inputs["L0_attn_W_Q"], "w_k0": inputs["L0_attn_W_K"],
        "w_v0": inputs["L0_attn_W_V"], "w_fc": inputs["L0_ffn_W_fc"],
        "ln_w": inputs["L0_ffn_ln_w"], "ln_b": inputs["L0_ffn_ln_b"],
        "w_proj": inputs["L0_ffn_W_proj"], "w_k1": inputs["L1_attn_W_K"],
        "w1a": inputs["L1_attn_W1"], "w2a": inputs["L1_attn_W2"],
        "wg": inputs["L1_ffn_Wg"], "we1": inputs["L1_ffn_W1"],
        "we2": inputs["L1_ffn_W2"],
        "tqc": cst["tqc"], "tqs": cst["tqs"], "tkc": cst["tkc"], "tks": cst["tks"],
        "ident": cst["ident"], "masks": cst["masks"], "ones1": cst["ones1"],
        "onesc": cst["onesc"], "consts": cst["consts"],
    }
    base = {k: np.ascontiguousarray(np.asarray(v)) for k, v in base.items()}
    in_maps = [dict(base, x_in=np.ascontiguousarray(xs[i])) for i in range(8)]
    res = run_bass_kernel_spmd(nc, in_maps, core_ids=list(range(8)))
    out = np.stack([np.asarray(r["y_out"], dtype=np.float32) for r in res.results], axis=0)
    return out


# version 09 (final)
# speedup vs baseline: 1.0435x; 1.0435x over previous
import numpy as np
import concourse.bass as bass
import concourse.mybir as mybir
from concourse.bass_utils import run_bass_kernel_spmd
from contextlib import ExitStack

F32 = mybir.dt.float32
BF16 = mybir.dt.bfloat16
AF = mybir.ActivationFunctionType
ALU = mybir.AluOpType
AX = mybir.AxisListType

T = 2048
D = 1024
HF = 4096
NCH = T // 128
DCH = D // 128
JCH = HF // 128
GAMMA = 0.96875
NEXP = 8
FEXP = 2048

PHASES = "all"

ENGS = ("sy", "gp", "ve", "sc", "te")


class Bld:

    def __init__(self, sem_names):
        self.ops = {e: [] for e in ENGS}
        self.sem_names = list(sem_names)
        self.sems = {}
        self.cnt = {s: 0 for s in sem_names}
        self.wm = {e: {} for e in ENGS}

    def wait(self, eng, sem, val):
        if val is None or val <= 0:
            return
        if sem == eng:
            return
        if val <= self.wm[eng].get(sem, 0):
            return
        self.wm[eng][sem] = val
        self.ops[eng].append(lambda E, s=sem, v=val: E.wait_ge(self.sems[s], v))

    def op(self, eng, fn, inc=None, n=1):
        if inc is not None:
            self.cnt[inc] += n
            def run(E, fn=fn, s=inc, n=n):
                fn(E).then_inc(self.sems[s], n)
            self.ops[eng].append(run)
            return self.cnt[inc]
        self.ops[eng].append(lambda E, fn=fn: fn(E))
        return None

    def barrier(self):
        for e in ENGS:
            for s in self.sem_names:
                self.wait(e, s, self.cnt[s])

    def dma(self, eng, qsem, out, in_, slow=False):
        if slow:
            return self.op(eng, lambda E, o=out, i=in_: E.dma_start(
                out=o, in_=i, allow_slow_non_contiguous=True), inc=qsem, n=16)
        return self.op(eng, lambda E, o=out, i=in_: E.dma_start(out=o, in_=i),
                       inc=qsem, n=16)


def _host_constants():
    d = D
    half = d // 2
    pos = np.arange(T, dtype=np.float64)
    scale_vec = (np.arange(0, d, 2, dtype=np.float64) + 0.4 * d) / (1.4 * d)
    scale = scale_vec[None, :] ** (pos[:, None] / 512.0)
    inv_freq = 1.0 / 10000.0 ** (np.arange(half, dtype=np.float64) / half)
    sinus = pos[:, None] * inv_freq[None, :]
    sin_u = np.repeat(np.sin(sinus) * scale, 2, axis=-1)
    cos_u = np.repeat(np.cos(sinus) * scale, 2, axis=-1)
    sin_d = np.repeat(np.sin(sinus) / scale, 2, axis=-1)
    cos_d = np.repeat(np.cos(sinus) / scale, 2, axis=-1)
    gpos = GAMMA ** pos
    gneg = GAMMA ** (-pos)

    def fold(s):
        out = s.copy()
        out[:, 0::2] = -s[:, 0::2]
        return out

    import ml_dtypes
    cst = dict(
        tqc=(cos_u * gpos[:, None]).astype(np.float32),
        tqs=(fold(sin_u) * gpos[:, None]).astype(np.float32),
        tkc=(cos_d * gneg[:, None]).astype(np.float32),
        tks=(fold(sin_d) * gneg[:, None]).astype(np.float32),
        ident=np.eye(128, dtype=ml_dtypes.bfloat16),
    )
    q = np.arange(512)[None, :]
    p = np.arange(128)[:, None]
    cst["masks"] = np.concatenate(
        [(q >= 128 * r + p) for r in range(4)], 0).astype(ml_dtypes.bfloat16)
    cst["ones1"] = np.ones((1, 128), dtype=np.float32)
    cst["onesc"] = np.ones((128, 1), dtype=np.float32)
    consts = np.zeros((128, 16), dtype=np.float32)
    consts[:, 0] = 1e-5
    consts[:, 1] = 1e-6
    cst["consts"] = consts
    return cst


def build(nc, dbg=False):
    X = nc.dram_tensor("x_in", [T, D], F32, kind="ExternalInput").ap()
    WQ = nc.dram_tensor("w_q", [D, D], F32, kind="ExternalInput").ap()
    WK0 = nc.dram_tensor("w_k0", [D, D], F32, kind="ExternalInput").ap()
    WV0 = nc.dram_tensor("w_v0", [D, D], F32, kind="ExternalInput").ap()
    WFC = nc.dram_tensor("w_fc", [HF, D], F32, kind="ExternalInput").ap()
    LNWI = nc.dram_tensor("ln_w", [HF], F32, kind="ExternalInput").ap()
    LNBI = nc.dram_tensor("ln_b", [HF], F32, kind="ExternalInput").ap()
    WPJ = nc.dram_tensor("w_proj", [D, HF], F32, kind="ExternalInput").ap()
    WK1 = nc.dram_tensor("w_k1", [D, D], F32, kind="ExternalInput").ap()
    W1A = nc.dram_tensor("w1a", [64, D], F32, kind="ExternalInput").ap()
    W2A = nc.dram_tensor("w2a", [D, 64], F32, kind="ExternalInput").ap()
    WG = nc.dram_tensor("wg", [D, NEXP], F32, kind="ExternalInput").ap()
    WE1 = nc.dram_tensor("we1", [NEXP, D, FEXP], F32, kind="ExternalInput").ap()
    WE2 = nc.dram_tensor("we2", [NEXP, FEXP, D], F32, kind="ExternalInput").ap()
    TQC = nc.dram_tensor("tqc", [T, D], F32, kind="ExternalInput").ap()
    TQS = nc.dram_tensor("tqs", [T, D], F32, kind="ExternalInput").ap()
    TKC = nc.dram_tensor("tkc", [T, D], F32, kind="ExternalInput").ap()
    TKS = nc.dram_tensor("tks", [T, D], F32, kind="ExternalInput").ap()
    IDENTI = nc.dram_tensor("ident", [128, 128], BF16, kind="ExternalInput").ap()
    MASKSI = nc.dram_tensor("masks", [4 * 128, 512], BF16, kind="ExternalInput").ap()
    ONES1I = nc.dram_tensor("ones1", [1, 128], F32, kind="ExternalInput").ap()
    ONESCI = nc.dram_tensor("onesc", [128, 1], F32, kind="ExternalInput").ap()
    CONSTI = nc.dram_tensor("consts", [128, 16], F32, kind="ExternalInput").ap()
    Y = nc.dram_tensor("y_out", [T, D], F32, kind="ExternalOutput").ap()
    WFCT = nc.dram_tensor("s_wfct", [8, 128, 8 * 512], BF16, kind="Internal").ap()
    WPJT = nc.dram_tensor("s_wpjt", [8, 128, 4 * 1024], BF16, kind="Internal").ap()
    QFH = nc.dram_tensor("s_qfh", [DCH, 128, T], BF16, kind="Internal").ap()
    QFL = nc.dram_tensor("s_qfl", [DCH, 128, T], BF16, kind="Internal").ap()
    KFH = nc.dram_tensor("s_kfh", [DCH, 128, T], BF16, kind="Internal").ap()
    KFL = nc.dram_tensor("s_kfl", [DCH, 128, T], BF16, kind="Internal").ap()
    VTH = nc.dram_tensor("s_vth", [NCH, 128, D], BF16, kind="Internal").ap()
    VTL = nc.dram_tensor("s_vtl", [NCH, 128, D], BF16, kind="Internal").ap()

    b = Bld(("qw0", "qw1", "qx0", "qx1", "qsp", "qsp1", "qo", "te", "ve", "sc", "gp2"))
    es = ExitStack()
    sbuf = lambda name, shape, dt: es.enter_context(nc.sbuf_tensor(name, shape, dt))
    psum = lambda name, shape, dt: es.enter_context(nc.psum_tensor(name, shape, dt))

    ACC = sbuf("ACC", [128, NCH * D], F32)
    XNH = sbuf("XNH", [128, DCH * T], BF16)
    XNL = sbuf("XNL", [128, DCH * T], BF16)
    PBF = sbuf("PBF", [128, 20480], BF16)
    PF3 = sbuf("PF3", [128, 6144], F32)
    IDE = sbuf("IDE", [128, 128], BF16)
    STAT = sbuf("STAT", [128, 64], F32)
    LNW = sbuf("LNW", [128, 32], F32)
    LNB = sbuf("LNB", [128, 32], F32)
    LNW2 = sbuf("LNW2", [128, 64], BF16)
    CONST = sbuf("CONST", [128, 16], F32)
    ONE1 = sbuf("ONE1", [1, 128], F32)
    ONEC = sbuf("ONEC", [128, 1], F32)
    B12 = sbuf("B12", [1, 2 * D], F32)
    SG1 = sbuf("SG1", [128, 128], F32)
    SG2 = sbuf("SG2", [128, 128], F32)
    CGT = sbuf("CGT", [128, 128], F32)
    GTH = sbuf("GTH", [128, 16], F32)
    VJ = sbuf("VJ", [128, 512], F32)
    PRP = sbuf("PRP", [128, 1024], BF16)

    PH = [psum("PH0", [128, 512], F32), psum("PH1", [128, 512], F32)]
    PY = [psum("PY0", [128, 1024], F32), psum("PY1", [128, 1024], F32)]
    PT = psum("PT", [128, 2048], BF16)

    XNH3 = XNH[:].rearrange("p (e t) -> p e t", e=DCH)
    XNL3 = XNL[:].rearrange("p (e t) -> p e t", e=DCH)

    def accs(c, a=0, n=D):
        return ACC[:, c * D + a: c * D + a + n]

    def vgap(n=1):
        for _ in range(n):
            b.op("ve", lambda E: E.tensor_scalar(VJ[:], VJ[:], 1.0, None, ALU.mult))

    b.dma("sy", "qw0", IDE[:], IDENTI)
    b.dma("sy", "qw0", CONST[:], CONSTI)
    b.dma("sy", "qw0", ONE1[:], ONES1I)
    b.dma("sy", "qw0", ONEC[:], ONESCI)
    b.dma("sy", "qw0", LNW[:], LNWI.rearrange("(c p) -> p c", p=128), slow=True)
    c_init = b.dma("sy", "qw0", LNB[:], LNBI.rearrange("(c p) -> p c", p=128), slow=True)

    b.op("ve", lambda E: E.memset(VJ[:], 0.0))
    b.wait("ve", "qw0", c_init)
    LNW2r = LNW2[:].rearrange("p (c k) -> p c k", k=2)
    b.op("ve", lambda E: E.tensor_copy(LNW2r[:, :, 0:1], LNW[:].rearrange("p (c k) -> p c k", k=1)))
    v_lnw2 = b.op("ve", lambda E: E.tensor_copy(
        LNW2r[:, :, 1:2], LNB[:].rearrange("p (c k) -> p c k", k=1)), inc="ve")

    XT = [PF3[:, 0:1024], PF3[:, 1024:2048]]
    SQJ = PF3[:, 2048:3072]

    def stats_finalize(n_feat):
        b.op("ve", lambda E: E.tensor_scalar(STAT[:, 32:48], STAT[:, 0:16], 1.0 / n_feat, None, ALU.mult))
        b.op("ve", lambda E: E.tensor_scalar(STAT[:, 48:64], STAT[:, 16:32], 1.0 / n_feat, None, ALU.mult))
        vgap()
        b.op("ve", lambda E: E.tensor_tensor(STAT[:, 0:16], STAT[:, 32:48], STAT[:, 32:48], ALU.mult))
        vgap()
        vv = b.op("ve", lambda E: E.tensor_tensor(STAT[:, 48:64], STAT[:, 48:64], STAT[:, 0:16], ALU.subtract), inc="ve")
        b.wait("sc", "ve", vv)
        sv = b.op("sc", lambda E: E.activation(STAT[:, 48:64], STAT[:, 48:64], AF.Sqrt, bias=CONST[:, 0:1]), inc="sc")
        b.wait("ve", "sc", sv)
        b.op("ve", lambda E: E.reciprocal(STAT[:, 48:64], STAT[:, 48:64]))
        vgap()
        b.op("ve", lambda E: E.tensor_tensor(STAT[:, 0:16], STAT[:, 32:48], STAT[:, 48:64], ALU.mult))
        vgap()
        return b.op("ve", lambda E: E.tensor_scalar(STAT[:, 0:16], STAT[:, 0:16], -1.0, None, ALU.mult), inc="ve")

    def ln_stats(from_hbm, pump=None):
        xt_free = [None, None]
        for c in range(NCH):
            if pump is not None:
                pump(c)
            pb = c % 2
            if from_hbm:
                q = "qx0" if pb == 0 else "qx1"
                if xt_free[pb]:
                    b.wait("sy", *xt_free[pb])
                cq = b.dma("sy", q, XT[pb][:], X[c * 128:(c + 1) * 128, :])
                src = XT[pb][:]
                b.wait("ve", q, cq)
            else:
                src = accs(c)
            b.op("ve", lambda E, c=c, src=src: E.tensor_reduce(
                STAT[:, c:c + 1], src, AX.X, ALU.add), inc="ve")
            b.op("ve", lambda E, src=src: E.tensor_tensor(SQJ[:], src, src, ALU.mult))
            v = b.op("ve", lambda E, c=c: E.tensor_reduce(
                STAT[:, 16 + c:17 + c], SQJ[:], AX.X, ALU.add), inc="ve")
            xt_free[pb] = ("ve", v)
        vgap()
        return stats_finalize(D)

    def ln_passB(from_hbm, v_stats):
        XNT = [PF3[:, 2048:3072], PF3[:, 3072:4096]]
        XHb = [PBF[:, 0:1024], PBF[:, 2048:3072]]
        XLb = [PBF[:, 1024:2048], PBF[:, 3072:4096]]
        xnt_free = [("ve", v_stats), ("ve", v_stats)]
        xh_free = [("te", 0), ("te", 0)]
        pt_free = ("ve", 0)
        xt_free = [("ve", v_stats), ("ve", v_stats)]
        b.wait("sc", "ve", v_stats)
        last_ve = 0
        for c in range(NCH):
            pb = c % 2
            if from_hbm:
                q = "qx0" if pb == 0 else "qx1"
                b.wait("sy", *xt_free[pb])
                cq = b.dma("sy", q, XT[pb][:], X[c * 128:(c + 1) * 128, :])
                b.wait("sc", q, cq)
                src = XT[pb][:]
            else:
                src = accs(c)
            b.wait("sc", *xnt_free[pb])
            s_norm = b.op("sc", lambda E, c=c, pb=pb, src=src: E.activation(
                XNT[pb][:], src, AF.Identity,
                scale=STAT[:, 48 + c:49 + c], bias=STAT[:, c:c + 1]), inc="sc")
            xt_free[pb] = ("sc", s_norm)
            b.wait("ve", "sc", s_norm)
            b.wait("ve", *xh_free[pb])
            b.op("ve", lambda E, pb=pb: E.tensor_copy(XHb[pb][:], XNT[pb][:]))
            v_split = b.op("ve", lambda E, pb=pb: E.tensor_tensor(
                XLb[pb][:], XNT[pb][:], XHb[pb][:], ALU.subtract), inc="ve")
            xnt_free[pb] = ("ve", v_split)
            b.wait("te", "ve", v_split)
            b.wait("te", *pt_free)
            b.wait("te", "qw0", c_init)
            for e in range(DCH):
                b.op("te", lambda E, e=e, pb=pb: E.transpose(
                    PT[:, e * 128:(e + 1) * 128],
                    XHb[pb][:, e * 128:(e + 1) * 128], IDE[:]))
            for e in range(DCH):
                kw = {"inc": "te"} if e == DCH - 1 else {}
                b.op("te", lambda E, e=e, pb=pb: E.transpose(
                    PT[:, 1024 + e * 128:1024 + (e + 1) * 128],
                    XLb[pb][:, e * 128:(e + 1) * 128], IDE[:]), **kw)
            t_tr = b.cnt["te"]
            xh_free[pb] = ("te", t_tr)
            b.wait("ve", "te", t_tr)
            b.op("ve", lambda E, c=c: E.tensor_copy(
                XNH3[:, :, c * 128:(c + 1) * 128],
                PT[:, 0:1024].rearrange("p (e t) -> p e t", e=DCH)))
            last_ve = b.op("ve", lambda E, c=c: E.tensor_copy(
                XNL3[:, :, c * 128:(c + 1) * 128],
                PT[:, 1024:2048].rearrange("p (e t) -> p e t", e=DCH)), inc="ve")
            pt_free = ("ve", last_ve)
        return last_ve

    W32G = [PF3[:, 4096:5120], PF3[:, 5120:6144]]
    gam_free = [("ve", 0), ("ve", 0)]

    def gamma_tile(k):
        if k < 32:
            ap = WFC[k * 128:(k + 1) * 128, :]
            gs = 0 + k
        else:
            kk = k - 32
            ap = WPJ[(kk // 4) * 128:(kk // 4 + 1) * 128, (kk % 4) * 1024:(kk % 4 + 1) * 1024]
            gs = 40 + kk
        pb = k % 2
        q = "qw0" if pb == 0 else "qw1"
        b.wait("sy", *gam_free[pb])
        cq = b.dma("sy", q, W32G[pb][:], ap)
        b.wait("ve", q, cq)
        v = b.op("ve", lambda E, gs=gs, pb=pb: E.tensor_reduce(
            SG2[:, gs:gs + 1], W32G[pb][:], AX.X, ALU.add,
            apply_absolute_value=True), inc="ve")
        gam_free[pb] = ("ve", v)

    def gamma_fin(col, gs_col):
        vgap()
        v = b.op("ve", lambda E, gs=gs_col: E.tensor_reduce(
            GTH[:, 4:5], SG2[:, gs:gs + 32], AX.X, ALU.add), inc="ve")
        vgap()
        b.wait("te", "ve", b.cnt["ve"])
        t1 = b.op("te", lambda E: E.matmul(PH[0][0:1, 0:1], ONEC[:], GTH[:, 4:5],
                                           start=True, stop=True), inc="te")
        b.wait("ve", "te", t1)
        v = b.op("ve", lambda E: E.tensor_copy(GTH[0:1, 5:6], PH[0][0:1, 0:1]), inc="ve")
        b.wait("te", "ve", v)
        t2 = b.op("te", lambda E: E.matmul(PH[1][:, 0:1], ONE1[:], GTH[0:1, 5:6],
                                           start=True, stop=True), inc="te")
        b.wait("ve", "te", t2)
        b.op("ve", lambda E: E.tensor_scalar(GTH[:, 6:7], PH[1][:, 0:1], 0.5 / (HF * D), None, ALU.mult))
        vgap()
        b.op("ve", lambda E, col=col: E.tensor_scalar(GTH[:, col:col + 1], GTH[:, 6:7], 0.5e-6, None, ALU.add))
        vgap()
        return b.op("ve", lambda E, col=col: E.tensor_scalar(
            GTH[:, col + 1:col + 2], GTH[:, col:col + 1], -1.0, None, ALU.mult), inc="ve")


    v_stats = ln_stats(True, pump=lambda c: [gamma_tile(4 * c + i) for i in range(4)])
    v_xn = ln_passB(True, v_stats)
    v_tfc = gamma_fin(0, 0)
    v_tpj = gamma_fin(2, 40)
    b.barrier()
    if dbg:
        XNHD = nc.dram_tensor("xnh_dbg", [128, DCH * T], BF16, kind="ExternalOutput").ap()
        XNLD = nc.dram_tensor("xnl_dbg", [128, DCH * T], BF16, kind="ExternalOutput").ap()
        STATD = nc.dram_tensor("stat_dbg", [128, 64], F32, kind="ExternalOutput").ap()
        b.dma("sy", "qo", XNHD, XNH[:])
        b.dma("sy", "qo", XNLD, XNL[:])
        b.dma("sy", "qo", STATD, STAT[:])
        b.wait("sy", "qo", b.cnt["qo"])
        b.barrier()

    W32 = [PF3[:, 4096:5120], PF3[:, 5120:6144]]

    QTB = [PBF[:, 4096:5120], PBF[:, 5120:6144]]
    P12 = [PF3[:, 0:1024], PF3[:, 1024:2048]]
    STG = [PBF[:, 6144:7168], PBF[:, 7168:8192]]

    def quant_tile(pb, thr_col, v_thr):
        b.op("ve", lambda E, pb=pb, tc=thr_col: E.tensor_scalar(
            P12[pb][:], W32[pb][:], GTH[:, tc + 1:tc + 2], None, ALU.is_lt))
        return b.op("ve", lambda E, pb=pb, tc=thr_col: E.scalar_tensor_tensor(
            QTB[pb][:], W32[pb][:], GTH[:, tc:tc + 1], P12[pb][:],
            ALU.is_gt, ALU.subtract), inc="ve")

    WFCT4 = WFCT.rearrange("jt p (e j) -> jt p e j", e=DCH)
    b.barrier()
    WPJT4 = WPJT.rearrange("jt p (kk o) -> jt p kk o", kk=4)

    def prep_spill_wpj():
        beta_ev = [0]
        w_free = [("ve", 0), ("ve", 0)]
        qt_free = [("te", 0), ("te", 0)]
        pt_free2 = [("ve", 0), ("ve", 0)]
        stg_free = [[], []]
        for k in range(32):
            pb = k % 2
            jq, ob = k // 8, k % 8
            q = "qx0" if pb == 0 else "qx1"
            b.wait("sy", *w_free[pb])
            cq = b.dma("sy", q, W32[pb][:],
                       WPJ[ob * 128:(ob + 1) * 128, jq * 1024:(jq + 1) * 1024])
            b.wait("ve", q, cq)
            b.wait("ve", *qt_free[pb])
            v_q = quant_tile(pb, 2, v_tpj)
            w_free[pb] = ("ve", v_q)
            b.wait("te", "ve", v_q)
            b.wait("te", *pt_free2[pb])
            for e in range(DCH):
                kw = {"inc": "te"} if e == DCH - 1 else {}
                b.op("te", lambda E, e=e, pb=pb: E.transpose(
                    PT[:, pb * 1024 + e * 128:pb * 1024 + (e + 1) * 128],
                    QTB[pb][:, e * 128:(e + 1) * 128], IDE[:]), **kw)
            t_tr = b.cnt["te"]
            qt_free[pb] = ("te", t_tr)
            b.wait("ve", "te", t_tr)
            for w in stg_free[pb]:
                b.wait("ve", *w)
            v_e = b.op("ve", lambda E, pb=pb: E.tensor_copy(
                STG[pb][:], PT[:, pb * 1024:(pb + 1) * 1024]), inc="ve")
            pt_free2[pb] = ("ve", v_e)
            b.wait("gp", "ve", v_e)
            qs = "qsp" if pb == 0 else "qsp1"
            b.dma("gp", qs,
                  WPJT4[2 * jq, :, 0:4, ob * 128:(ob + 1) * 128],
                  STG[pb][:, 0:512].rearrange("p (e j) -> p e j", e=4))
            cs = b.dma("gp", qs,
                       WPJT4[2 * jq + 1, :, 0:4, ob * 128:(ob + 1) * 128],
                       STG[pb][:, 512:1024].rearrange("p (e j) -> p e j", e=4))
            b.wait("te", "ve", v_e)
            b.wait("te", "ve", v_lnw2)
            b.wait("te", "ve", beta_ev[0])
            for e in range(DCH):
                jc = 8 * jq + e
                b.op("te", lambda E, e=e, jc=jc, pb=pb, ob=ob,
                     st=(e == 0), sp=(e == DCH - 1): E.matmul(
                    PY[0][0:1, ob * 128:(ob + 1) * 128], LNW2[:, 2 * jc:2 * jc + 1],
                    STG[pb][:, e * 128:(e + 1) * 128], start=st, stop=sp))
                kw = {"inc": "te"} if e == DCH - 1 else {}
                b.op("te", lambda E, e=e, jc=jc, pb=pb, ob=ob,
                     st=(e == 0), sp=(e == DCH - 1): E.matmul(
                    PY[1][0:1, ob * 128:(ob + 1) * 128], LNW2[:, 2 * jc + 1:2 * jc + 2],
                    STG[pb][:, e * 128:(e + 1) * 128], start=st, stop=sp), **kw)
            t_beta = b.cnt["te"]
            b.wait("ve", "te", t_beta)
            if jq == 0:
                b.op("ve", lambda E, ob=ob: E.tensor_copy(
                    B12[0:1, ob * 128:(ob + 1) * 128], PY[0][0:1, ob * 128:(ob + 1) * 128]))
                v_bev = b.op("ve", lambda E, ob=ob: E.tensor_copy(
                    B12[0:1, D + ob * 128:D + (ob + 1) * 128],
                    PY[1][0:1, ob * 128:(ob + 1) * 128]), inc="ve")
            else:
                b.op("ve", lambda E, ob=ob: E.tensor_tensor(
                    B12[0:1, ob * 128:(ob + 1) * 128], B12[0:1, ob * 128:(ob + 1) * 128],
                    PY[0][0:1, ob * 128:(ob + 1) * 128], ALU.add))
                v_bev = b.op("ve", lambda E, ob=ob: E.tensor_tensor(
                    B12[0:1, D + ob * 128:D + (ob + 1) * 128],
                    B12[0:1, D + ob * 128:D + (ob + 1) * 128],
                    PY[1][0:1, ob * 128:(ob + 1) * 128], ALU.add), inc="ve")
            beta_ev[0] = v_bev
            stg_free[pb] = [("te", t_beta), (qs, cs)]
        return b.cnt["qsp"]

    b.barrier()
    c_spill = prep_spill_wpj()
    b.barrier()
    v_beta = b.cnt["ve"]

    W32F = PF3[:, 0:1024]
    P2F = PF3[:, 1024:2048]
    QTBF = PRP[:, 0:1024]
    STGF = PRP[:, 0:1024]
    fcp = {"w32": ("ve", 0), "qtb": ("te", 0), "stg": None, "ptp": ("ve", 0),
           "load": 0, "vq": 0, "p2c": 0}

    def fc_stepA(k):
        pb = k % 2
        q = "qw0" if pb == 0 else "qw1"
        b.wait("sy", *fcp["w32"])
        cq = b.dma("sy", q, W32F[:], WFC[k * 128:(k + 1) * 128, :])
        b.wait("ve", q, cq)
        b.wait("ve", *fcp["qtb"])
        if fcp["stg"]:
            b.wait("ve", *fcp["stg"])
        b.op("ve", lambda E: E.tensor_scalar(P2F[:], W32F[:], GTH[:, 1:2], None, ALU.is_lt))
        v_q = b.op("ve", lambda E: E.scalar_tensor_tensor(
            QTBF[:], W32F[:], GTH[:, 0:1], P2F[:], ALU.is_gt, ALU.subtract), inc="ve")
        fcp["w32"] = ("ve", v_q)
        fcp["vq"] = v_q

    def fc_stepB(k):
        b.wait("te", "ve", fcp["vq"])
        b.wait("te", *fcp["ptp"])
        for e in range(DCH):
            kw = {"inc": "te"} if e == DCH - 1 else {}
            b.op("te", lambda E, e=e: E.transpose(
                PT[:, 1024 + e * 128:1024 + (e + 1) * 128],
                QTBF[:, e * 128:(e + 1) * 128], IDE[:]), **kw)
        t_tr = b.cnt["te"]
        fcp["qtb"] = ("te", t_tr)
        b.wait("ve", "te", t_tr)
        if fcp["stg"]:
            b.wait("ve", *fcp["stg"])
        v_e = b.op("ve", lambda E: E.tensor_copy(STGF[:], PT[:, 1024:2048]), inc="ve")
        fcp["ptp"] = ("ve", v_e)
        b.wait("gp", "ve", v_e)
        qs = "qsp" if k % 2 == 0 else "qsp1"
        cs = b.dma("gp", qs, WFCT4[k // 4, :, :, (k % 4) * 128:(k % 4 + 1) * 128],
                   STGF[:].rearrange("p (e j) -> p e j", e=DCH))
        fcp["stg"] = (qs, cs)

    from collections import deque
    fc_steps = deque()
    for _k in range(32):
        fc_steps.append((fc_stepA, _k))
        fc_steps.append((fc_stepB, _k))

    def pump_fc():
        if fc_steps:
            f, k = fc_steps.popleft()
            f(k)

    for _ in range(8):
        pump_fc()

    B1B = PF3[:, 4096:5120]
    B2B = PF3[:, 5120:6144]
    b.wait("te", "ve", v_beta)
    tb = b.op("te", lambda E: E.matmul(PY[0][:, 0:512], ONE1[:], B12[0:1, 0:512], start=True, stop=True))
    tb = b.op("te", lambda E: E.matmul(PY[0][:, 512:1024], ONE1[:], B12[0:1, 512:1024], start=True, stop=True))
    tb = b.op("te", lambda E: E.matmul(PY[1][:, 0:512], ONE1[:], B12[0:1, 1024:1536], start=True, stop=True))
    tb = b.op("te", lambda E: E.matmul(PY[1][:, 512:1024], ONE1[:], B12[0:1, 1536:2048], start=True, stop=True), inc="te")
    b.wait("ve", "te", tb)
    b.op("ve", lambda E: E.tensor_copy(B1B, PY[0][:]))
    v_bb = b.op("ve", lambda E: E.tensor_copy(B2B, PY[1][:]), inc="ve")

    WFB = [PBF[:, 0:4096], PBF[:, 4096:8192]]
    WPB = [PBF[:, 8192:12288], PBF[:, 12288:16384]]
    GHB = [PBF[:, 16384:16896], PBF[:, 16896:17408]]
    GLB = [PBF[:, 17408:17920], PBF[:, 17920:18432]]
    GFH = [PBF[:, 18432:18944], PBF[:, 18944:19456]]
    GFL = [PBF[:, 19456:19968], PBF[:, 19968:20480]]
    GT = [PF3[:, 2048:2560], PF3[:, 2560:3072]]
    SQ2 = PF3[:, 3072:3584]

    wf_free = [("te", 0), ("te", 0)]
    wp_free = [("te", 0), ("te", 0)]
    ph_free = [("ve", v_bb), ("ve", v_bb)]
    gt_free = [("ve", 0), ("ve", 0)]
    gh_free = [("te", 0), ("te", 0)]
    ptf_free = [("ve", 0), ("ve", 0)]
    gf_free = [("te", 0), ("te", 0)]
    py_free = [("ve", v_bb), ("ve", v_bb)]
    z_done = [0] * NCH

    def ffn_tr(jt, c):
        pb = c % 2
        v_sp = split_ms[pb]
        b.wait("te", "ve", v_sp)
        b.wait("te", *ptf_free[0])
        for k in range(4):
            b.op("te", lambda E, k=k, pb=pb: E.transpose(
                PT[:, k * 128:(k + 1) * 128],
                GHB[pb][:, k * 128:(k + 1) * 128], IDE[:]))
        for k in range(4):
            kw = {"inc": "te"} if k == 3 else {}
            b.op("te", lambda E, k=k, pb=pb: E.transpose(
                PT[:, 512 + k * 128:512 + (k + 1) * 128],
                GLB[pb][:, k * 128:(k + 1) * 128], IDE[:]), **kw)
        t_tr = b.cnt["te"]
        gh_free[pb] = ("te", t_tr)
        b.wait("ve", "te", t_tr)
        b.wait("ve", *gf_free[pb])
        for k in range(4):
            jc = 4 * jt + k
            b.op("ve", lambda E, k=k, pb=pb, jc=jc: E.tensor_scalar(
                GFH[pb][:, k * 128:(k + 1) * 128],
                PT[:, k * 128:(k + 1) * 128],
                LNW[:, jc:jc + 1], None, ALU.mult))
        for k in range(4):
            jc = 4 * jt + k
            kw = {"inc": "ve"} if k == 3 else {}
            b.op("ve", lambda E, k=k, pb=pb, jc=jc: E.tensor_scalar(
                GFL[pb][:, k * 128:(k + 1) * 128],
                PT[:, 512 + k * 128:512 + (k + 1) * 128],
                LNW[:, jc:jc + 1], None, ALU.mult), **kw)
        gf_ready[pb] = b.cnt["ve"]
        ptf_free[0] = ("ve", gf_ready[pb])

    def ffn_z(jt, c):
        pb = c % 2
        pj = jt % 2
        b.wait("te", "ve", gf_ready[pb])
        b.wait("te", *py_free[pb])
        for oh in range(2):
            for k in range(4):
                for hp, GSRC in ((0, GFH), (1, GFL)):
                    last = (oh == 1 and k == 3 and hp == 1)
                    kw = {"inc": "te"} if last else {}
                    b.op("te", lambda E, oh=oh, k=k, pb=pb, pj=pj, GSRC=GSRC,
                         st=(k == 0 and hp == 0), sp=(k == 3 and hp == 1): E.matmul(
                        PY[pb][:, oh * 512:(oh + 1) * 512],
                        GSRC[pb][:, k * 128:(k + 1) * 128],
                        WPB[pj][:, k * 1024 + oh * 512:k * 1024 + (oh + 1) * 512],
                        start=st, stop=sp), **kw)
        t_z = b.cnt["te"]
        gf_free[pb] = ("te", t_z)
        wp_free[pj] = ("te", t_z)
        b.wait("ve", "te", t_z)
        if jt == 0:
            v_z = b.op("ve", lambda E, c=c, pb=pb: E.tensor_copy(accs(c), PY[pb][:]), inc="ve")
        else:
            v_z = b.op("ve", lambda E, c=c, pb=pb: E.tensor_tensor(
                accs(c), accs(c), PY[pb][:], ALU.add), inc="ve")
        z_done[c] = v_z
        py_free[pb] = ("ve", v_z)

    split_ms = [0, 0]
    gf_ready = [0, 0]
    pend_tr = None
    pend_z = None
    for jt in range(8):
        pj = jt % 2
        qj = "qw0" if pj == 0 else "qw1"
        b.wait("sy", "qsp", b.cnt["qsp"])
        b.wait("sy", "qsp1", b.cnt["qsp1"])
        b.wait("sy", *wf_free[pj])
        b.wait("sy", *wp_free[pj])
        cw = b.dma("sy", qj, WFB[pj][:], WFCT[jt])
        cw = b.dma("sy", qj, WPB[pj][:], WPJT[jt])
        for c in range(NCH):
            pb = c % 2
            b.wait("te", qj, cw)
            b.wait("te", "ve", v_xn)
            b.wait("te", *ph_free[pb])
            for e in range(DCH):
                for hp, XSRC in ((0, XNH), (1, XNL)):
                    kw = {"inc": "te"} if (e == DCH - 1 and hp == 1) else {}
                    b.op("te", lambda E, e=e, c=c, pb=pb, pj=pj, XSRC=XSRC,
                         st=(e == 0 and hp == 0), sp=(e == DCH - 1 and hp == 1): E.matmul(
                        PH[pb][:], XSRC[:, e * T + c * 128:e * T + (c + 1) * 128],
                        WFB[pj][:, e * 512:(e + 1) * 512], start=st, stop=sp), **kw)
            t_h = b.cnt["te"]
            wf_free[pj] = ("te", t_h)
            if pend_tr is not None:
                ffn_tr(*pend_tr)
            if pend_z is not None:
                ffn_z(*pend_z)
            pend_z = pend_tr
            pump_fc()
            b.wait("sc", "te", t_h)
            b.wait("sc", *gt_free[pb])
            col = c * 8 + jt
            b.op("sc", lambda E, pb=pb, col=col: E.activation(
                GT[pb][:], PH[pb][:], AF.Gelu, accum_out=SG1[:, col:col + 1]))
            s_g = b.op("sc", lambda E, pb=pb, col=col: E.activation(
                SQ2[:], GT[pb][:], AF.Square, accum_out=SG2[:, col:col + 1]), inc="sc")
            ph_free[pb] = ("sc", s_g)
            b.wait("sc", *gh_free[pb])
            s_hi = b.op("sc", lambda E, pb=pb: E.activation(
                GHB[pb][:], GT[pb][:], AF.Identity), inc="sc")
            ph_free[pb] = ("sc", s_hi)
            b.wait("ve", "sc", s_hi)
            v_sp = b.op("ve", lambda E, pb=pb: E.tensor_tensor(
                GLB[pb][:], GT[pb][:], GHB[pb][:], ALU.subtract), inc="ve")
            gt_free[pb] = ("ve", v_sp)
            split_ms[pb] = v_sp
            pend_tr = (jt, c)
    ffn_tr(*pend_tr)
    if pend_z is not None:
        ffn_z(*pend_z)
    ffn_z(*pend_tr)

    for _ in range(8):
        b.op("sc", lambda E: E.activation(GTH[:, 9:10], CONST[:, 0:1], AF.Square))
    b.op("sc", lambda E: E.activation(GTH[:, 9:10], CONST[:, 0:1], AF.Square), inc="sc")
    b.barrier()
    b.wait("ve", "sc", b.cnt["sc"])
    for c in range(NCH):
        b.op("ve", lambda E, c=c: E.tensor_reduce(
            STAT[:, c:c + 1], SG1[:, c * 8:c * 8 + 8], AX.X, ALU.add))
    for c in range(NCH):
        b.op("ve", lambda E, c=c: E.tensor_reduce(
            STAT[:, 16 + c:17 + c], SG2[:, c * 8:c * 8 + 8], AX.X, ALU.add))
    b.op("ve", lambda E: E.tensor_scalar(VJ[:], VJ[:], 1.0, None, ALU.mult), inc="ve")
    v_fs = stats_finalize(HF)
    XTF = [PF3[:, 2048:3072], PF3[:, 3072:4096]]
    xt_free = [("ve", 0), ("ve", 0)]
    v_fin = 0
    for c in range(NCH):
        pb = c % 2
        q = "qx0" if pb == 0 else "qx1"
        b.wait("sy", *xt_free[pb])
        cq = b.dma("sy", q, XTF[pb][:], X[c * 128:(c + 1) * 128, :])
        b.wait("ve", q, cq)
        b.wait("ve", "ve", max(v_fs, v_bb))
        b.op("ve", lambda E, c=c, pb=pb: E.scalar_tensor_tensor(
            accs(c), accs(c), STAT[:, 48 + c:49 + c], XTF[pb][:], ALU.mult, ALU.add))
        b.op("ve", lambda E, c=c: E.scalar_tensor_tensor(
            accs(c), B1B, STAT[:, c:c + 1], accs(c), ALU.mult, ALU.add))
        v_fin = b.op("ve", lambda E, c=c: E.tensor_tensor(
            accs(c), accs(c), B2B, ALU.add), inc="ve")
        xt_free[pb] = ("ve", v_fin)

    if PHASES != "ffn0":
        WHq2 = [PBF[:, 0:4096], PBF[:, 8192:12288]]
        WLq2 = [PBF[:, 4096:8192], PBF[:, 12288:16384]]
        QHT = [PBF[:, 16384:16896], PBF[:, 17408:17920]]
        QLT = [PBF[:, 16896:17408], PBF[:, 17920:18432]]
        STGQ = [PBF[:, 18432:19456], PBF[:, 19456:20480]]
        W32S = [PF3[:, 0:512], PF3[:, 512:1024]]
        TBLC = [PF3[:, 1024:1536], PF3[:, 4096:4608]]
        TBLS = [PF3[:, 1536:2048], PF3[:, 4608:5120]]
        SWP = [PF3[:, 5120:5632], PF3[:, 5632:6144]]
        QXM = [PF3[:, 2048:2560], PF3[:, 2560:3072]]

        wh_free2 = [("te", 0), ("te", 0)]
        swp_scfree = [[], []]
        w32_free = [None, None]
        qt_free = [[], []]
        stg_free = [[], []]
        ph_free2 = [[], []]
        pt_free3 = [("ve", 0), ("ve", 0)]
        tbl_free = [[("ve", v_fin)], [("ve", v_fin)]]

        def qk_tail(pb, kind, ot, c, v_sp):
            b.wait("te", "ve", v_sp)
            b.wait("te", *pt_free3[pb])
            for k in range(4):
                b.op("te", lambda E, k=k, pb=pb: E.transpose(
                    PT[:, pb * 1024 + k * 128:pb * 1024 + (k + 1) * 128],
                    QHT[pb][:, k * 128:(k + 1) * 128], IDE[:]))
            for k in range(4):
                kw = {"inc": "te"} if k == 3 else {}
                b.op("te", lambda E, k=k, pb=pb: E.transpose(
                    PT[:, pb * 1024 + 512 + k * 128:pb * 1024 + 512 + (k + 1) * 128],
                    QLT[pb][:, k * 128:(k + 1) * 128], IDE[:]), **kw)
            t_tr = b.cnt["te"]
            qt_free[pb] = [("te", t_tr)]
            b.wait("ve", "te", t_tr)
            for w in stg_free[pb]:
                b.wait("ve", *w)
            v_st = b.op("ve", lambda E, pb=pb: E.tensor_copy(
                STGQ[pb][:], PT[:, pb * 1024:(pb + 1) * 1024]), inc="ve")
            pt_free3[pb] = ("ve", v_st)
            b.wait("gp", "ve", v_st)
            qs = "qsp" if pb == 0 else "qsp1"
            DH, DL = (QFH, QFL) if kind == "q" else (KFH, KFL)
            b.dma("gp", qs,
                  DH[ot * 4:(ot + 1) * 4, :, c * 128:(c + 1) * 128].rearrange("k p t -> p k t"),
                  STGQ[pb][:, 0:512].rearrange("p (k t) -> p k t", k=4))
            cs = b.dma("gp", qs,
                       DL[ot * 4:(ot + 1) * 4, :, c * 128:(c + 1) * 128].rearrange("k p t -> p k t"),
                       STGQ[pb][:, 512:1024].rearrange("p (k t) -> p k t", k=4))
            stg_free[pb] = [(qs, cs)]

        qk_pending = None
        slices = [(mi, ot) for mi in range(3) for ot in range(2)]
        wsrcs = (WQ, WK0, WV0)
        kinds = ("q", "k", "v")
        slice_info = {}

        def qkv_load_slice(si):
            mi, ot = slices[si]
            sb_ = si % 2
            WSRC = wsrcs[mi]
            b.wait("gp", *wh_free2[sb_])
            if si < 2:
                b.wait("gp", *wf_free[0])
                b.wait("gp", *wf_free[1])
                b.wait("gp", *wp_free[0])
                b.wait("gp", *wp_free[1])
                b.wait("gp", *gh_free[0])
                b.wait("gp", *gh_free[1])
                b.wait("gp", *gf_free[0])
                b.wait("gp", *gf_free[1])
            ch = b.dma("gp", "qsp", WHq2[sb_][:].rearrange("p (e o) -> p e o", e=DCH),
                       WSRC[:, ot * 512:(ot + 1) * 512].rearrange("(e p) o -> p e o", e=DCH))
            v_lo = 0
            for qtr in range(8):
                pq = qtr % 2
                qq = "qw0" if pq == 0 else "qw1"
                if w32_free[pq]:
                    b.wait("sy", *w32_free[pq])
                b.wait("sy", *wh_free2[sb_])
                cq0 = b.dma("sy", qq, W32S[pq][:],
                            WSRC[qtr * 128:(qtr + 1) * 128, ot * 512:(ot + 1) * 512])
                b.wait("ve", qq, cq0)
                b.wait("ve", "qsp", ch)
                v_lo = b.op("ve", lambda E, pq=pq, qtr=qtr, sb_=sb_: E.tensor_tensor(
                    WLq2[sb_][:, qtr * 512:(qtr + 1) * 512], W32S[pq][:],
                    WHq2[sb_][:, qtr * 512:(qtr + 1) * 512], ALU.subtract), inc="ve")
                w32_free[pq] = ("ve", v_lo)
            slice_info[si] = (ch, v_lo)

        qkv_load_slice(0)
        for si in range(6):
            mi, ot = slices[si]
            kind = kinds[mi]
            WSRC = wsrcs[mi]
            sb_ = si % 2
            ch, v_lo = slice_info[si]
            WHq = WHq2[sb_]
            WLq = WLq2[sb_]
            if si + 1 < 6:
                qkv_load_slice(si + 1)
            if True:
                for c in range(NCH):
                    pb = c % 2
                    ct = None
                    if kind in ("q", "k"):
                        TC, TS = (TQC, TQS) if kind == "q" else (TKC, TKS)
                        qq = "qx0" if pb == 0 else "qx1"
                        for w in tbl_free[pb]:
                            b.wait("sy", *w)
                        b.dma("sy", qq, TBLC[pb][:], TC[c * 128:(c + 1) * 128, ot * 512:(ot + 1) * 512])
                        ct = b.dma("sy", qq, TBLS[pb][:], TS[c * 128:(c + 1) * 128, ot * 512:(ot + 1) * 512])
                    b.wait("te", "qsp", ch)
                    b.wait("te", "ve", v_lo)
                    for w in ph_free2[pb]:
                        b.wait("te", *w)
                    n_mm = 0
                    for e in range(DCH):
                        for XS, WS in ((XNH, WHq), (XNL, WHq), (XNH, WLq)):
                            n_mm += 1
                            kw = {"inc": "te"} if n_mm == 24 else {}
                            b.op("te", lambda E, e=e, c=c, pb=pb, XS=XS, WS=WS,
                                 st=(n_mm == 1), sp=(n_mm == 24): E.matmul(
                                PH[pb][:], XS[:, e * T + c * 128:e * T + (c + 1) * 128],
                                WS[:, e * 512:(e + 1) * 512], start=st, stop=sp), **kw)
                    t_p = b.cnt["te"]
                    wh_free2[sb_] = ("te", t_p)
                    if qk_pending is not None:
                        qk_tail(*qk_pending)
                        qk_pending = None
                    b.wait("ve", "te", t_p)
                    for w in qt_free[pb]:
                        b.wait("ve", *w)
                    if kind == "v":
                        b.op("ve", lambda E, pb=pb: E.tensor_copy(QHT[pb][:], PH[pb][:]))
                        v_ev = b.op("ve", lambda E, pb=pb: E.tensor_tensor(
                            QLT[pb][:], PH[pb][:], QHT[pb][:], ALU.subtract), inc="ve")
                        ph_free2[pb] = [("ve", v_ev)]
                        b.wait("gp", "ve", v_ev)
                        qs = "qsp" if pb == 0 else "qsp1"
                        b.dma("gp", qs, VTH[c, :, ot * 512:(ot + 1) * 512], QHT[pb][:])
                        cs = b.dma("gp", qs, VTL[c, :, ot * 512:(ot + 1) * 512], QLT[pb][:])
                        qt_free[pb] = [(qs, cs)]
                    else:
                        b.wait("sc", "te", t_p)
                        for w in swp_scfree[pb]:
                            b.wait("sc", *w)
                        s_sw = b.op("sc", lambda E, pb=pb: E.activation(
                            SWP[pb][:].rearrange("p (o k) -> p o k", k=2)[:, :, 0:1],
                            PH[pb][:].rearrange("p (o k) -> p o k", k=2)[:, :, 1:2],
                            AF.Identity), inc="sc")
                        b.wait("ve", qq, ct)
                        b.op("ve", lambda E, pb=pb: E.tensor_copy(
                            SWP[pb][:].rearrange("p (o k) -> p o k", k=2)[:, :, 1:2],
                            PH[pb][:].rearrange("p (o k) -> p o k", k=2)[:, :, 0:1]))
                        b.op("ve", lambda E, pb=pb: E.tensor_tensor(
                            QXM[pb][:], PH[pb][:], TBLC[pb][:], ALU.mult))
                        b.wait("ve", "sc", s_sw)
                        v_ph = b.op("ve", lambda E, pb=pb: E.tensor_tensor(
                            SWP[pb][:], SWP[pb][:], TBLS[pb][:], ALU.mult), inc="ve")
                        ph_free2[pb] = [("ve", v_ph)]
                        v_qx = b.op("ve", lambda E, pb=pb: E.tensor_tensor(
                            QXM[pb][:], QXM[pb][:], SWP[pb][:], ALU.add), inc="ve")
                        b.wait("sc", "ve", v_qx)
                        s_hi = b.op("sc", lambda E, pb=pb: E.activation(
                            QHT[pb][:], QXM[pb][:], AF.Identity), inc="sc")
                        b.wait("ve", "sc", s_hi)
                        v_sp = b.op("ve", lambda E, pb=pb: E.tensor_tensor(
                            QLT[pb][:], QXM[pb][:], QHT[pb][:], ALU.subtract), inc="ve")
                        tbl_free[pb] = [("ve", v_sp)]
                        swp_scfree[pb] = [("ve", v_qx)]
                        qk_pending = (pb, kind, ot, c, v_sp)

        if qk_pending is not None:
            qk_tail(*qk_pending)
            qk_pending = None

    if PHASES != "ffn0":
        b.barrier()
        QFB_H = PBF[:, 0:2048]
        QFB_L = PBF[:, 2048:4096]
        KFB_H = [PBF[:, 4096:5120], PBF[:, 6144:7168]]
        KFB_L = [PBF[:, 5120:6144], PBF[:, 7168:8192]]
        VFB_H = [PBF[:, 8192:9216], PBF[:, 10240:11264]]
        VFB_L = [PBF[:, 9216:10240], PBF[:, 11264:12288]]
        SB_H = PBF[:, 16384:17408]
        SB_L = PBF[:, 17408:18432]
        MASKT = PBF[:, 18432:18944]
        SMT = PF3[:, 0:512]

        cm = b.dma("sy", "qw0", MASKT[:].rearrange("p (r q) -> p r q", r=2),
                   MASKSI.rearrange("(r p) q -> p r q", r=4)[:, 0:2, 0:256])
        qf_free = ("te", 0)
        kf_free = [[], []]
        vf_free = [[], []]
        sb_free = ("te", 0)
        ph_free3 = [[], []]
        for j8 in range(8):
            b.wait("sy", *qf_free)
            b.dma("sy", "qw0", QFB_H[:].rearrange("p (e t) -> p e t", e=DCH),
                  QFH[:, :, j8 * 256:(j8 + 1) * 256].rearrange("e p t -> p e t"))
            cq1 = b.dma("sy", "qw0", QFB_L[:].rearrange("p (e t) -> p e t", e=DCH),
                        QFL[:, :, j8 * 256:(j8 + 1) * 256].rearrange("e p t -> p e t"))
            n_mc_tot = 2 * j8 + 2
            nmq = (n_mc_tot + 3) // 4
            for mq in range(nmq):
                mcs = list(range(mq * 4, min(mq * 4 + 4, n_mc_tot)))
                for mi_, mc in enumerate(mcs):
                    pk = mc % 2
                    qk = "qx0" if pk == 0 else "qx1"
                    for w in kf_free[pk]:
                        b.wait("sy", *w)
                    b.dma("sy", qk, KFB_H[pk][:].rearrange("p (e t) -> p e t", e=DCH),
                          KFH[:, :, mc * 128:(mc + 1) * 128].rearrange("e p t -> p e t"))
                    ck = b.dma("sy", qk, KFB_L[pk][:].rearrange("p (e t) -> p e t", e=DCH),
                               KFL[:, :, mc * 128:(mc + 1) * 128].rearrange("e p t -> p e t"))
                    b.wait("te", qk, ck)
                    b.wait("te", "qw0", cq1)
                    for w in ph_free3[pk]:
                        b.wait("te", *w)
                    n_mm = 0
                    for e in range(DCH):
                        for KS, QS in ((KFB_H[pk], QFB_H), (KFB_H[pk], QFB_L), (KFB_L[pk], QFB_H)):
                            n_mm += 1
                            kw = {"inc": "te"} if n_mm == 24 else {}
                            b.op("te", lambda E, e=e, pk=pk, KS=KS, QS=QS,
                                 st=(n_mm == 1), sp=(n_mm == 24): E.matmul(
                                PH[pk][:, 0:256], KS[:, e * 128:(e + 1) * 128],
                                QS[:, e * 256:(e + 1) * 256], start=st, stop=sp), **kw)
                    t_s = b.cnt["te"]
                    kf_free[pk] = [("te", t_s)]
                    b.wait("ve", "te", t_s)
                    b.wait("ve", *sb_free)
                    b.wait("ve", "qw0", cm)
                    r = mc - 2 * j8
                    sl = slice(mi_ * 256, (mi_ + 1) * 256)
                    if r >= 0:
                        b.op("ve", lambda E, pk=pk, r=r: E.tensor_tensor(
                            SMT[:, 0:256], PH[pk][:, 0:256], MASKT[:, r * 256:(r + 1) * 256], ALU.mult))
                        b.op("ve", lambda E, sl=sl: E.tensor_copy(SB_H[:, sl], SMT[:, 0:256]))
                        v_se = b.op("ve", lambda E, sl=sl: E.tensor_tensor(
                            SB_L[:, sl], SMT[:, 0:256], SB_H[:, sl], ALU.subtract), inc="ve")
                    else:
                        b.op("ve", lambda E, pk=pk, sl=sl: E.tensor_copy(SB_H[:, sl], PH[pk][:, 0:256]))
                        v_se = b.op("ve", lambda E, pk=pk, sl=sl: E.tensor_tensor(
                            SB_L[:, sl], PH[pk][:, 0:256], SB_H[:, sl], ALU.subtract), inc="ve")
                    ph_free3[pk] = [("ve", v_se)]
                b.wait("te", "ve", b.cnt["ve"])
                for mi_, mc in enumerate(mcs):
                    pv = mc % 2
                    qv = "qx0" if pv == 0 else "qx1"
                    for w in vf_free[pv]:
                        b.wait("sy", *w)
                    b.dma("sy", qv, VFB_H[pv][:], VTH[mc])
                    cv = b.dma("sy", qv, VFB_L[pv][:], VTL[mc])
                    b.wait("te", qv, cv)
                    for oh in range(2):
                        for ns in range(2):
                            for pi, (SS, VS) in enumerate(
                                    ((SB_H, VFB_H[pv]), (SB_L, VFB_H[pv]), (SB_H, VFB_L[pv]))):
                                first = (mq == 0 and mi_ == 0 and pi == 0)
                                last = (mq == nmq - 1 and mi_ == len(mcs) - 1 and pi == 2)
                                kw = {"inc": "te"} if (last and oh == 1 and ns == 1) else {}
                                b.op("te", lambda E, mi_=mi_, oh=oh, ns=ns, SS=SS, VS=VS,
                                     st=first, sp=last: E.matmul(
                                    PY[ns][:, oh * 512:(oh + 1) * 512],
                                    SS[:, mi_ * 256 + ns * 128:mi_ * 256 + (ns + 1) * 128],
                                    VS[:, oh * 512:(oh + 1) * 512], start=st, stop=sp), **kw)
                    vf_free[pv] = [("te", b.cnt["te"])]
                sb_free = ("te", b.cnt["te"])
            t_out = b.cnt["te"]
            qf_free = ("te", t_out)
            b.wait("ve", "te", t_out)
            for ns in range(2):
                c = 2 * j8 + ns
                kw = {"inc": "ve"} if ns == 1 else {}
                b.op("ve", lambda E, ns=ns, c=c: E.tensor_tensor(
                    accs(c), accs(c), PY[ns][:], ALU.add), **kw)
            b.wait("te", "ve", b.cnt["ve"])

    if PHASES not in ("ffn0", "l0"):
        b.barrier()
        v_stats2 = ln_stats(False)
        v_xn2 = ln_passB(False, v_stats2)
        b.barrier()

        LG = PF3[:, 0:128]
        EA = PF3[:, 128:256]
        EB = PF3[:, 256:384]
        WG32 = PF3[:, 384:448]
        M1 = GTH[:, 0:16]
        M2 = STAT[:, 0:16]
        DEN = STAT[:, 16:32]
        WGB_H = PBF[:, 0:64]
        WGB_L = PBF[:, 64:128]
        cg = b.dma("sy", "qw0", WG32[:].rearrange("p (e g) -> p e g", e=DCH),
                   WG.rearrange("(e p) g -> p e g", e=DCH))
        b.wait("ve", "qw0", cg)
        b.op("ve", lambda E: E.tensor_copy(WGB_H[:], WG32[:]))
        v_wg = b.op("ve", lambda E: E.tensor_tensor(WGB_L[:], WG32[:], WGB_H[:], ALU.subtract), inc="ve")
        b.wait("te", "ve", max(v_wg, v_xn2))
        lg_free = [[], []]
        for c in range(NCH):
            pb = c % 2
            for w in lg_free[pb]:
                b.wait("te", *w)
            n_mm = 0
            for e in range(DCH):
                for XS, WS in ((XNH, WGB_H), (XNL, WGB_H), (XNH, WGB_L)):
                    n_mm += 1
                    kw = {"inc": "te"} if n_mm == 24 else {}
                    b.op("te", lambda E, e=e, c=c, pb=pb, XS=XS, WS=WS,
                         st=(n_mm == 1), sp=(n_mm == 24): E.matmul(
                        PH[pb][:, 0:8], XS[:, e * T + c * 128:e * T + (c + 1) * 128],
                        WS[:, e * 8:(e + 1) * 8], start=st, stop=sp), **kw)
            t_g = b.cnt["te"]
            b.wait("ve", "te", t_g)
            v_c = b.op("ve", lambda E, c=c, pb=pb: E.tensor_copy(
                LG[:, c * 8:(c + 1) * 8], PH[pb][:, 0:8]), inc="ve")
            lg_free[pb] = [("ve", v_c)]
        LG3 = LG.rearrange("p (c g) -> p c g", g=NEXP)
        EA3 = EA.rearrange("p (c g) -> p c g", g=NEXP)
        EB3 = EB.rearrange("p (c g) -> p c g", g=NEXP)
        M1u = M1.rearrange("p (c u) -> p c u", u=1).broadcast_to([128, NCH, NEXP])
        M2u = M2.rearrange("p (c u) -> p c u", u=1).broadcast_to([128, NCH, NEXP])
        DENu = DEN.rearrange("p (c u) -> p c u", u=1).broadcast_to([128, NCH, NEXP])
        vgap()
        b.op("ve", lambda E: E.tensor_reduce(
            M1.rearrange("p (c u) -> p c u", u=1), LG3, AX.X, ALU.max))
        vgap()
        v1 = b.op("ve", lambda E: E.tensor_tensor(EA3, LG3, M1u, ALU.subtract), inc="ve")
        b.wait("sc", "ve", v1)
        s_e = b.op("sc", lambda E: E.activation(EA[:], EA[:], AF.Exp), inc="sc")
        b.op("ve", lambda E: E.tensor_tensor(EB3, LG3, M1u, ALU.is_ge))
        vgap()
        b.op("ve", lambda E: E.tensor_scalar(EB[:], EB[:], 1e30, None, ALU.mult))
        vgap()
        b.op("ve", lambda E: E.tensor_tensor(EB[:], LG[:], EB[:], ALU.subtract))
        vgap()
        b.op("ve", lambda E: E.tensor_reduce(
            M2.rearrange("p (c u) -> p c u", u=1), EB3, AX.X, ALU.max))
        vgap()
        b.op("ve", lambda E: E.tensor_tensor(EB3, LG3, M2u, ALU.is_ge))
        b.wait("ve", "sc", s_e)
        vgap()
        b.op("ve", lambda E: E.tensor_tensor(EA[:], EA[:], EB[:], ALU.mult))
        vgap()
        b.op("ve", lambda E: E.tensor_reduce(
            DEN.rearrange("p (c u) -> p c u", u=1), EA3, AX.X, ALU.add))
        vgap()
        b.op("ve", lambda E: E.reciprocal(DEN[:], DEN[:]))
        vgap()
        v_cgt = b.op("ve", lambda E: E.tensor_tensor(
            CGT[:].rearrange("p (c g) -> p c g", g=NEXP), EA3, DENu, ALU.mult), inc="ve")

        b.barrier()
        W1S = PF3[:, 0:1024]
        WKS = [PF3[:, 1024:1536], PF3[:, 1536:2048]]
        AFB = PF3[:, 2048:2560]
        W1H = PBF[:, 128:1152]
        W1L = PBF[:, 1152:2176]
        W1TH = PBF[:, 2176:2688]
        W1TL = PBF[:, 2688:3200]
        MTH = PBF[:, 3200:3712]
        MTL = PBF[:, 3712:4224]
        W2TH = PBF[:, 4224:5248]
        W2TL = PBF[:, 5248:6272]
        WKH = [PBF[:, 6272:6784], PBF[:, 7296:7808]]
        WKL = [PBF[:, 6784:7296], PBF[:, 7808:8320]]
        MHB = PBF[:, 8320:9344]
        MLB = PBF[:, 9344:10368]
        AFH = PBF[:, 10368:10880]
        AFL = PBF[:, 10880:11392]

        cw1 = b.dma("sy", "qw0", W1S[0:64, :], W1A)
        b.wait("ve", "qw0", cw1)
        b.op("ve", lambda E: E.tensor_copy(W1H[0:64, :], W1S[0:64, :]))
        v_w1 = b.op("ve", lambda E: E.tensor_tensor(
            W1L[0:64, :], W1S[0:64, :], W1H[0:64, :], ALU.subtract), inc="ve")
        b.wait("te", "ve", v_w1)
        for e in range(DCH):
            b.op("te", lambda E, e=e: E.transpose(
                PT[:, e * 64:(e + 1) * 64], W1H[0:64, e * 128:(e + 1) * 128], IDE[0:64, 0:64]))
        for e in range(DCH):
            kw = {"inc": "te"} if e == DCH - 1 else {}
            b.op("te", lambda E, e=e: E.transpose(
                PT[:, 512 + e * 64:512 + (e + 1) * 64],
                W1L[0:64, e * 128:(e + 1) * 128], IDE[0:64, 0:64]), **kw)
        t_w1t = b.cnt["te"]
        b.wait("ve", "te", t_w1t)
        b.op("ve", lambda E: E.tensor_copy(W1TH[:], PT[:, 0:512]))
        v_w1t = b.op("ve", lambda E: E.tensor_copy(W1TL[:], PT[:, 512:1024]), inc="ve")
        wks_free = [[], []]
        for oh in range(2):
            for kc in range(DCH):
                pk = kc % 2
                qq = "qx0" if pk == 0 else "qx1"
                for w in wks_free[pk]:
                    b.wait("sy", *w)
                ckk = b.dma("sy", qq, WKS[pk][:], WK1[kc * 128:(kc + 1) * 128, oh * 512:(oh + 1) * 512])
                b.wait("ve", qq, ckk)
                b.op("ve", lambda E, pk=pk: E.tensor_copy(WKH[pk][:], WKS[pk][:]))
                v_wk = b.op("ve", lambda E, pk=pk: E.tensor_tensor(
                    WKL[pk][:], WKS[pk][:], WKH[pk][:], ALU.subtract), inc="ve")
                b.wait("te", "ve", max(v_wk, v_w1t))
                n_mm = 0
                for W1T_, WKX in ((W1TH, WKH[pk]), (W1TL, WKH[pk]), (W1TH, WKL[pk])):
                    n_mm += 1
                    kw = {"inc": "te"} if (kc == DCH - 1 and n_mm == 3) else {}
                    b.op("te", lambda E, kc=kc, oh=oh, W1T_=W1T_, WKX=WKX,
                         st=(kc == 0 and n_mm == 1), sp=(kc == DCH - 1 and n_mm == 3): E.matmul(
                        PY[oh][0:64, 0:512], W1T_[:, kc * 64:(kc + 1) * 64],
                        WKX[:], start=st, stop=sp), **kw)
                wks_free[pk] = [("te", b.cnt["te"]), ("ve", v_wk)]
        t_m = b.cnt["te"]
        b.wait("ve", "te", t_m)
        b.op("ve", lambda E: E.tensor_copy(MHB[0:64, 0:512], PY[0][0:64, 0:512]))
        b.op("ve", lambda E: E.tensor_copy(MHB[0:64, 512:1024], PY[1][0:64, 0:512]))
        b.op("ve", lambda E: E.tensor_tensor(
            MLB[0:64, 0:512], PY[0][0:64, 0:512], MHB[0:64, 0:512], ALU.subtract))
        v_m = b.op("ve", lambda E: E.tensor_tensor(
            MLB[0:64, 512:1024], PY[1][0:64, 0:512], MHB[0:64, 512:1024], ALU.subtract), inc="ve")
        b.wait("te", "ve", v_m)
        for e in range(DCH):
            b.op("te", lambda E, e=e: E.transpose(
                PT[:, 1024 + e * 64:1024 + (e + 1) * 64], MHB[0:64, e * 128:(e + 1) * 128], IDE[0:64, 0:64]))
        for e in range(DCH):
            kw = {"inc": "te"} if e == DCH - 1 else {}
            b.op("te", lambda E, e=e: E.transpose(
                PT[:, 1536 + e * 64:1536 + (e + 1) * 64], MLB[0:64, e * 128:(e + 1) * 128], IDE[0:64, 0:64]), **kw)
        t_mt = b.cnt["te"]
        b.wait("ve", "te", t_mt)
        b.op("ve", lambda E: E.tensor_copy(MTH[:], PT[:, 1024:1536]))
        v_mt = b.op("ve", lambda E: E.tensor_copy(MTL[:], PT[:, 1536:2048]), inc="ve")
        w2_parts = []
        for hb in range(DCH):
            pk = hb % 2
            qq = "qx0" if pk == 0 else "qx1"
            for w in wks_free[pk]:
                b.wait("sy", *w)
            ck2 = b.dma("sy", qq, WKS[pk][:, 0:64], W2A[hb * 128:(hb + 1) * 128, :])
            b.wait("ve", qq, ck2)
            b.wait("ve", "te", t_mt)
            b.op("ve", lambda E, pk=pk: E.tensor_copy(WKH[pk][:, 0:64], WKS[pk][:, 0:64]))
            v_w2 = b.op("ve", lambda E, pk=pk: E.tensor_tensor(
                WKL[pk][:, 0:64], WKS[pk][:, 0:64], WKH[pk][:, 0:64], ALU.subtract), inc="ve")
            b.wait("te", "ve", max(v_w2, v_mt))
            b.op("te", lambda E, pk=pk, hb=hb: E.transpose(
                PT[0:64, hb * 128:(hb + 1) * 128], WKH[pk][:, 0:64], IDE[:]))
            kw = {"inc": "te"} if hb == DCH - 1 else {}
            b.op("te", lambda E, pk=pk, hb=hb: E.transpose(
                PT[0:64, 1024 + hb * 128:1024 + (hb + 1) * 128], WKL[pk][:, 0:64], IDE[:]), **kw)
            wks_free[pk] = [("te", b.cnt["te"]), ("ve", v_w2)]
        t_w2t = b.cnt["te"]
        b.wait("ve", "te", t_w2t)
        b.op("ve", lambda E: E.tensor_copy(W2TH[0:64, :], PT[0:64, 0:1024]))
        v_w2t = b.op("ve", lambda E: E.tensor_copy(W2TL[0:64, :], PT[0:64, 1024:2048]), inc="ve")
        b.wait("te", "ve", max(v_mt, v_w2t))
        af_free = []
        afb_free = []
        ry_free = [[], []]
        for tt in range(4):
            pb = tt % 2
            n_mm = 0
            for e in range(DCH):
                for MT_, XS in ((MTH, XNH), (MTH, XNL), (MTL, XNH)):
                    n_mm += 1
                    kw = {"inc": "te"} if n_mm == 24 else {}
                    b.op("te", lambda E, e=e, tt=tt, pb=pb, MT_=MT_, XS=XS,
                         st=(n_mm == 1), sp=(n_mm == 24): E.matmul(
                        PH[pb][0:64, 0:512], MT_[:, e * 64:(e + 1) * 64],
                        XS[:, e * T + tt * 512:e * T + (tt + 1) * 512], start=st, stop=sp), **kw)
            t_a = b.cnt["te"]
            b.wait("sc", "te", t_a)
            for w in afb_free:
                b.wait("sc", *w)
            s_a = b.op("sc", lambda E, pb=pb: E.activation(
                AFB[0:64, :], PH[pb][0:64, 0:512], AF.Gelu), inc="sc")
            b.wait("ve", "sc", s_a)
            for w in af_free:
                b.wait("ve", *w)
            b.op("ve", lambda E: E.tensor_copy(AFH[0:64, :], AFB[0:64, :]))
            v_af = b.op("ve", lambda E: E.tensor_tensor(
                AFL[0:64, :], AFB[0:64, :], AFH[0:64, :], ALU.subtract), inc="ve")
            b.wait("te", "ve", v_af)
            for idx, (ns, oh) in enumerate([(n_, o_) for n_ in range(4) for o_ in range(2)]):
                pr = idx % 2
                for w in ry_free[pr]:
                    b.wait("te", *w)
                n_mm = 0
                for AS, WS in ((AFH, W2TH), (AFL, W2TH), (AFH, W2TL)):
                    n_mm += 1
                    kw = {"inc": "te"} if n_mm == 3 else {}
                    b.op("te", lambda E, ns=ns, oh=oh, pr=pr, AS=AS, WS=WS,
                         st=(n_mm == 1), sp=(n_mm == 3): E.matmul(
                        PY[pr][:, 0:512], AS[0:64, ns * 128:(ns + 1) * 128],
                        WS[0:64, oh * 512:(oh + 1) * 512], start=st, stop=sp), **kw)
                t_r = b.cnt["te"]
                b.wait("ve", "te", t_r)
                c = tt * 4 + ns
                v_r = b.op("ve", lambda E, c=c, oh=oh, pr=pr: E.tensor_tensor(
                    accs(c, oh * 512, 512), accs(c, oh * 512, 512), PY[pr][:, 0:512], ALU.add), inc="ve")
                ry_free[pr] = [("ve", v_r)]
            af_free = [("te", b.cnt["te"])]
            afb_free = [("ve", v_af)]

        moe_on = PHASES != "reads"
        b.barrier()
        W1BH = PBF[:, 0:2048]
        W1BL = PBF[:, 2048:4096]
        W2BH = PBF[:, 4096:6144]
        W2BL = PBF[:, 6144:8192]
        HFB_H = PBF[:, 8192:12288]
        HFB_L = PBF[:, 12288:16384]
        WST = PF3[:, 0:2048]
        w1_free = []
        w2_free = []
        wst_free = []
        hf_free = []
        ph_free4 = [[], []]
        py_free4 = [[], []]
        for ex in range(NEXP if moe_on else 0):
            for g in range(8):
                for w in w1_free:
                    b.wait("gp", *w)
                ch1 = b.dma("gp", "qsp", W1BH[:].rearrange("p (e f) -> p e f", e=DCH),
                            WE1[ex, :, g * 256:(g + 1) * 256].rearrange("(e p) f -> p e f", e=DCH))
                for w in wst_free:
                    b.wait("sy", *w)
                for w in w1_free:
                    b.wait("sy", *w)
                cq1 = b.dma("sy", "qw0", WST[:].rearrange("p (e f) -> p e f", e=DCH),
                            WE1[ex, :, g * 256:(g + 1) * 256].rearrange("(e p) f -> p e f", e=DCH))
                b.wait("ve", "qw0", cq1)
                b.wait("ve", "qsp", ch1)
                for w in w1_free:
                    b.wait("ve", *w)
                v_l1 = b.op("ve", lambda E: E.tensor_tensor(
                    W1BL[:], WST[:], W1BH[:], ALU.subtract), inc="ve")
                wst_free = [("ve", v_l1)]
                relu_ms = [0, 0, 0, 0]
                for tt in range(4):
                    for fc in range(2):
                        pb = (tt * 2 + fc) % 2
                        for w in ph_free4[pb]:
                            b.wait("te", *w)
                        b.wait("te", "ve", v_l1)
                        for w in hf_free:
                            b.wait("te", *w)
                        n_mm = 0
                        for e in range(DCH):
                            for WS, XS in ((W1BH, XNH), (W1BH, XNL), (W1BL, XNH)):
                                n_mm += 1
                                kw = {"inc": "te"} if n_mm == 24 else {}
                                b.op("te", lambda E, e=e, fc=fc, tt=tt, pb=pb, WS=WS, XS=XS,
                                     st=(n_mm == 1), sp=(n_mm == 24): E.matmul(
                                    PH[pb][:], WS[:, e * 256 + fc * 128:e * 256 + (fc + 1) * 128],
                                    XS[:, e * T + tt * 512:e * T + (tt + 1) * 512],
                                    start=st, stop=sp), **kw)
                        t_h = b.cnt["te"]
                        b.wait("ve", "te", t_h)
                        off = fc * 2048 + tt * 512
                        b.op("ve", lambda E, pb=pb, off=off: E.tensor_scalar(
                            HFB_H[:, off:off + 512], PH[pb][:], 0.0, None, ALU.max))
                        v_hf = b.op("ve", lambda E, pb=pb, off=off: E.scalar_tensor_tensor(
                            HFB_L[:, off:off + 512], PH[pb][:], 0.0,
                            HFB_H[:, off:off + 512], ALU.max, ALU.subtract), inc="ve")
                        ph_free4[pb] = [("ve", v_hf)]
                        relu_ms[tt] = v_hf
                v_hall = b.cnt["ve"]
                for w in w2_free:
                    b.wait("gp", *w)
                ch2 = b.dma("gp", "qsp1", W2BH[:].rearrange("p (e o) -> p e o", e=2),
                            WE2[ex, g * 256:(g + 1) * 256, :].rearrange("(e p) o -> p e o", e=2))
                for w in wst_free:
                    b.wait("sy", *w)
                for w in w2_free:
                    b.wait("sy", *w)
                cq2 = b.dma("sy", "qw1", WST[:].rearrange("p (e o) -> p e o", e=2),
                            WE2[ex, g * 256:(g + 1) * 256, :].rearrange("(e p) o -> p e o", e=2))
                b.wait("ve", "qw1", cq2)
                b.wait("ve", "qsp1", ch2)
                for w in w2_free:
                    b.wait("ve", *w)
                v_l2 = b.op("ve", lambda E: E.tensor_tensor(
                    W2BL[:], WST[:], W2BH[:], ALU.subtract), inc="ve")
                wst_free = [("ve", v_l2)]
                b.wait("te", "ve", max(v_l2, v_cgt))
                for c in range(NCH):
                    pb = c % 2
                    b.wait("te", "ve", relu_ms[c // 4])
                    for w in py_free4[pb]:
                        b.wait("te", *w)
                    n_mm = 0
                    for fc in range(2):
                        for HS, WS in ((HFB_H, W2BH), (HFB_L, W2BH), (HFB_H, W2BL)):
                            for oh in range(2):
                                n_mm += 1
                                kw = {"inc": "te"} if n_mm == 12 else {}
                                st_ = (fc == 0 and HS is HFB_H and WS is W2BH)
                                sp_ = (fc == 1 and HS is HFB_H and WS is W2BL)
                                b.op("te", lambda E, c=c, pb=pb, fc=fc, oh=oh, HS=HS, WS=WS,
                                     st=st_, sp=sp_: E.matmul(
                                    PY[pb][:, oh * 512:(oh + 1) * 512],
                                    HS[:, fc * 2048 + (c // 4) * 512 + (c % 4) * 128:
                                       fc * 2048 + (c // 4) * 512 + (c % 4 + 1) * 128],
                                    WS[:, fc * 1024 + oh * 512:fc * 1024 + (oh + 1) * 512],
                                    start=st, stop=sp), **kw)
                    t_y = b.cnt["te"]
                    b.wait("ve", "te", t_y)
                    v_y = b.op("ve", lambda E, c=c, pb=pb, ex=ex: E.scalar_tensor_tensor(
                        accs(c), PY[pb][:], CGT[:, c * NEXP + ex:c * NEXP + ex + 1],
                        accs(c), ALU.mult, ALU.add), inc="ve")
                    py_free4[pb] = [("ve", v_y)]
                hf_free = [("te", b.cnt["te"])]
                w1_free = [("te", t_h)]
                w2_free = [("te", b.cnt["te"])]

    b.wait("sy", "ve", b.cnt["ve"])
    co = b.dma("sy", "qo", Y.rearrange("(c p) d -> p c d", p=128),
               ACC[:].rearrange("p (c d) -> p c d", c=NCH))
    b.wait("sy", "qo", co)

    with ExitStack() as outer:
        for s in b.sem_names:
            b.sems[s] = outer.enter_context(nc.semaphore(s))
        blk = outer.enter_context(nc.Block())

        @blk.sync
        def _(E):
            for fn in b.ops["sy"]:
                fn(E)

        @blk.gpsimd
        def _(E):
            for fn in b.ops["gp"]:
                fn(E)

        @blk.vector
        def _(E):
            for fn in b.ops["ve"]:
                fn(E)

        @blk.scalar
        def _(E):
            for fn in b.ops["sc"]:
                fn(E)

        @blk.tensor
        def _(E):
            for fn in b.ops["te"]:
                fn(E)
    es.close()
    return nc


_CST = None


def kernel(**inputs):
    global _CST
    if _CST is None:
        _CST = _host_constants()
    cst = _CST
    nc = bass.Bass("TRN2", target_bir_lowering=False, detect_race_conditions=False)
    build(nc)
    xs = np.ascontiguousarray(inputs["L0_attn_X"], dtype=np.float32)
    base = {
        "w_q": inputs["L0_attn_W_Q"], "w_k0": inputs["L0_attn_W_K"],
        "w_v0": inputs["L0_attn_W_V"], "w_fc": inputs["L0_ffn_W_fc"],
        "ln_w": inputs["L0_ffn_ln_w"], "ln_b": inputs["L0_ffn_ln_b"],
        "w_proj": inputs["L0_ffn_W_proj"], "w_k1": inputs["L1_attn_W_K"],
        "w1a": inputs["L1_attn_W1"], "w2a": inputs["L1_attn_W2"],
        "wg": inputs["L1_ffn_Wg"], "we1": inputs["L1_ffn_W1"],
        "we2": inputs["L1_ffn_W2"],
        "tqc": cst["tqc"], "tqs": cst["tqs"], "tkc": cst["tkc"], "tks": cst["tks"],
        "ident": cst["ident"], "masks": cst["masks"], "ones1": cst["ones1"],
        "onesc": cst["onesc"], "consts": cst["consts"],
    }
    base = {k: np.ascontiguousarray(np.asarray(v)) for k, v in base.items()}
    in_maps = [dict(base, x_in=np.ascontiguousarray(xs[i])) for i in range(8)]
    res = run_bass_kernel_spmd(nc, in_maps, core_ids=list(range(8)))
    out = np.stack([np.asarray(r["y_out"], dtype=np.float32) for r in res.results], axis=0)
    return out
